# version 28
# baseline (speedup 1.0000x reference)
import numpy as np
import ml_dtypes
import concourse.bass as bass
import concourse.mybir as mybir
from concourse.bass_utils import run_bass_kernel_spmd
from concourse.aot_env import allow_ici_all_to_all

F32 = mybir.dt.float32
BF16 = mybir.dt.bfloat16
AF = mybir.ActivationFunctionType
ALU = mybir.AluOpType
AX = mybir.AxisListType

NCORES = 8
B = 2
S = 8192
E = 2048
EC = 16
PB = 1024
TPC = 2048
NTG = 4
SCALE = 2048.0 ** -0.25
ROPE_BASE = 10000.0
LN_EPS = 1e-5
NORM_EPS = 1e-12
BIG = 1.0e9
C_LNSC = 1.0 / 2048.0

_DBG = None


class Sem:
    def __init__(self, h):
        self.h = h
        self.n = 0

    def inc(self, inst, k=1):
        inst.then_inc(self.h, k)
        self.n += k

    def inc_default(self, inst):
        inst.then_inc(self.h)
        self.n += 1


def _q_row(j, sl):
    return 512 * j + 128 * sl


def _kv_row(j, sl, typ):
    return 1024 * j + 256 * sl + 128 * typ


def build():
    nc = bass.Bass("TRN2", target_bir_lowering=False)

    xT = nc.dram_tensor("xT", [B, E, PB], F32, kind="ExternalInput").ap()
    wqkvT = nc.dram_tensor("wqkvT", [E, 3 * E], F32, kind="ExternalInput").ap()
    woutT = nc.dram_tensor("woutT", [E, E], F32, kind="ExternalInput").ap()
    wupT = nc.dram_tensor("wupT", [E, E], F32, kind="ExternalInput").ap()
    wdnT = nc.dram_tensor("wdnT", [E, E], F32, kind="ExternalInput").ap()
    wexp = nc.dram_tensor("wexp", [E, E], F32, kind="ExternalInput").ap()
    wgT = nc.dram_tensor("wgT", [E, 8], F32, kind="ExternalInput").ap()
    bqkv2 = nc.dram_tensor("bqkv2", [128, 48], F32, kind="ExternalInput").ap()
    bout2 = nc.dram_tensor("bout2", [128, 16], F32, kind="ExternalInput").ap()
    ropem = nc.dram_tensor("ropem", [48, 128, 128], BF16, kind="ExternalInput").ap()
    masksd = nc.dram_tensor("masksd", [2, 128, 1024], BF16, kind="ExternalInput").ap()
    identd = nc.dram_tensor("identd", [128, 128], BF16, kind="ExternalInput").ap()
    onesd = nc.dram_tensor("onesd", [128, 128], BF16, kind="ExternalInput").ap()
    cstd = nc.dram_tensor("cstd", [128, 8], F32, kind="ExternalInput").ap()
    outT = nc.dram_tensor("outT", [B, E, PB], F32, kind="ExternalOutput").ap()

    qch = [nc.dram_tensor("qch0", [4096, 512], BF16).ap(),
           nc.dram_tensor("qch1", [4096, 512], BF16).ap()]
    qgh = [nc.dram_tensor("qgh0", [4096, 512], BF16).ap(),
           nc.dram_tensor("qgh1", [4096, 512], BF16).ap()]
    kvc = nc.dram_tensor("kvc", [8192, 1024], BF16).ap()
    kvg = nc.dram_tensor("kvg", [8192, 1024], BF16).ap()
    aoc0 = nc.dram_tensor("aoc0", [2048, 1024], BF16).ap()
    aog0 = nc.dram_tensor("aog0", [2048, 1024], BF16).ap()
    aoc1 = nc.dram_tensor("aoc1", [2048, 1024], BF16).ap()
    aog1 = nc.dram_tensor("aog1", [2048, 1024], BF16).ap()
    yscr = nc.dram_tensor("yscr", [E, TPC], F32).ap()

    kvg_f = kvg.rearrange("a b -> (a b)")
    kvc_f = kvc.rearrange("a b -> (a b)")

    sems = {}
    sem_names = ["ld_c", "ld_g", "ld_w0", "ld_w1", "ld_kv0", "ld_kv1",
                 "ld_q0", "ld_q1", "ld_x0", "ld_x1",
                 "st_m0", "st_m1", "st_v0", "st_v1", "st_y0", "st_y1",
                 "st_o0", "st_o1", "cc1", "cc2", "cc3", "cc4", "cc5",
                 "pe", "act", "dve"]

    import contextlib
    with contextlib.ExitStack() as stack:
        id_sb = stack.enter_context(nc.sbuf_tensor("id_sb", [128, 128], BF16))
        ones_sb = stack.enter_context(nc.sbuf_tensor("ones_sb", [128, 128], BF16))
        bq_sb = stack.enter_context(nc.sbuf_tensor("bq_sb", [128, 48], F32))
        bo_sb = stack.enter_context(nc.sbuf_tensor("bo_sb", [128, 16], F32))
        wg_sb = stack.enter_context(nc.sbuf_tensor("wg_sb", [128, 128], BF16))
        cst_sb = stack.enter_context(nc.sbuf_tensor("cst_sb", [128, 8], F32))
        for s_ in sem_names:
            sems[s_] = Sem(stack.enter_context(nc.semaphore(s_)))
        blk = stack.enter_context(nc.Block())

        ld_c, ld_g = sems["ld_c"], sems["ld_g"]
        ld_g2 = ld_c
        ld_w = [sems["ld_w0"], sems["ld_w1"]]
        ld_kv = [sems["ld_kv0"], sems["ld_kv1"]]
        ld_q = [sems["ld_q0"], sems["ld_q1"]]
        ld_x = [sems["ld_x0"], sems["ld_x1"]]
        st_m = [sems["st_m0"], sems["st_m1"]]
        st_v = [sems["st_v0"], sems["st_v1"]]
        st_y = [sems["st_y0"], sems["st_y1"]]
        st_o = [sems["st_o0"], sems["st_o1"]]
        cc1, cc2 = sems["cc1"], sems["cc2"]
        cc3, cc4, cc5 = sems["cc3"], sems["cc4"], sems["cc5"]
        pe, act, dve = sems["pe"], sems["act"], sems["dve"]

        onecol = ones_sb[:, 0:1]
        onerow = ones_sb[0:1, :]

        def full_barrier():
            targets = [(s_, s_.n) for name_, s_ in sems.items()
                       if s_.n and not name_.startswith("cc")]
            for section in (blk.tensor, blk.scalar, blk.vector, blk.gpsimd,
                            blk.sync):
                @section
                def _(eng):
                    for s_, n_ in targets:
                        eng.wait_ge(s_.h, n_)

        nc.const_aps.register(F32, C_LNSC, cst_sb[:, 0:1])
        nc.const_aps.register(F32, LN_EPS, cst_sb[:, 1:2])
        nc.const_aps.register(F32, -0.5, cst_sb[:, 2:3])
        nc.const_aps.register(F32, 0.5, cst_sb[:, 3:4])

        @blk.sync
        def _(sy):
            ld_c.inc(sy.dma_start(out=id_sb[:], in_=identd), 16)
            ld_c.inc(sy.dma_start(out=ones_sb[:], in_=onesd), 16)
            ld_c.inc(sy.dma_start(out=bq_sb[:], in_=bqkv2), 16)
            ld_c.inc(sy.dma_start(out=bo_sb[:], in_=bout2), 16)
            ld_c.inc(sy.dma_start(out=cst_sb[:], in_=cstd), 16)

        @blk.gpsimd
        def _(g):
            src = wgT.rearrange("(ec e) g -> e ec g", e=128)
            dst = wg_sb[:].rearrange("p (ec g) -> p ec g", g=8)
            ld_c.inc(g.dma_start(out=dst, in_=src), 16)

        CONST_LD = ld_c.n

        with contextlib.ExitStack() as pa:
            rm_sb = pa.enter_context(nc.sbuf_tensor("rm_sb", [128, 48 * 128], BF16))
            h_big = pa.enter_context(nc.sbuf_tensor("h_big", [128, EC * TPC], BF16))
            h_all = [h_big[:, TPC * i:TPC * i + TPC] for i in range(EC)]

            @blk.sync
            def _(sy):
                src = ropem.rearrange("m p c -> p m c")
                dst = rm_sb[:].rearrange("p (m c) -> p m c", c=128)
                ld_c.inc(sy.dma_start(out=dst, in_=src), 16)

            ROPEM_LD = ld_c.n

            def rm(i):
                return rm_sb[:, 128 * i:128 * i + 128]

            if True:
                pl = pa.enter_context(contextlib.ExitStack())
                xb_ab = [pl.enter_context(
                    nc.sbuf_tensor(f"xb_a{i}", [128, 512 * 16], BF16))
                    for i in range(2)]
                sq_a = pl.enter_context(nc.sbuf_tensor("sq_a", [128, 512 * 16], BF16))
                lnrow = pl.enter_context(nc.sbuf_tensor("lnrow", [1, 512 * 7], F32))
                lnrow_b = pl.enter_context(nc.sbuf_tensor("lnrow_b", [1, 1024], BF16))
                lnbc = pl.enter_context(nc.sbuf_tensor("lnbc", [128, 1024], BF16))
                htmp = pl.enter_context(nc.sbuf_tensor("htmp", [128, 512], BF16))
                stat_ps = pl.enter_context(nc.psum_tensor("stat_ps", [1, 512], F32))
                stat_s = stat_ps[0:1, :]
                stat_s2 = stat_ps[0:1, :]
                bc12 = pl.enter_context(nc.psum_tensor("bc12", [128, 1024], F32))
                bc1 = bc12[:, 0:512]
                bc2 = bc12[:, 512:1024]

                xb_a = xb_ab[0]

                def xb(e):
                    return xb_a[:, 512 * e:512 * e + 512]

                def sqt(e):
                    return sq_a[:, 512 * e:512 * e + 512]

                m_row = lnrow[:, 0:512]
                s2_row = lnrow[:, 512:1024]
                msq_row = lnrow[:, 1024:1536]
                var_row = lnrow[:, 1536:2048]
                lnt_row = lnrow[:, 2048:2560]
                rstd_row = lnrow[:, 2560:3072]
                mrs_row = lnrow[:, 3072:3584]
                rstd_b = lnrow_b[:, 0:512]
                mrs_b = lnrow_b[:, 512:1024]
                rstd_bc = lnbc[:, 0:512]
                mrs_bc = lnbc[:, 512:1024]

                STATS_PREV = 0
                H_TG = {}
                XB_FREE = [None, None]
                for tg in range(NTG):
                    b, hf = tg // 2, tg % 2
                    xb_a = xb_ab[tg % 2]

                    @blk.gpsimd
                    def _(g, b=b, hf=hf, XF=XB_FREE[tg % 2]):
                        if XF:
                            g.wait_ge(pe.h, XF[0])
                            g.wait_ge(dve.h, XF[1])
                        for e in range(EC):
                            ld_g.inc(g.dma_start(
                                out=xb(e),
                                in_=xT[b, 128 * e:128 * e + 128,
                                       512 * hf:512 * hf + 512]), 16)

                    LD_XB = ld_g.n

                    @blk.vector
                    def _(v, xb_a=xb_a):
                        v.wait_ge(ld_g.h, LD_XB)
                        if STATS_PREV:
                            v.wait_ge(pe.h, STATS_PREV)
                        dve.inc(v.tensor_tensor(out=sq_a[:], in0=xb_a[:],
                                                in1=xb_a[:], op=ALU.mult), 1)

                    SQ_DONE = dve.n

                    @blk.tensor
                    def _(t):
                        t.wait_ge(ld_c.h, CONST_LD)
                        t.wait_ge(ld_g.h, LD_XB)
                        if dve.n:
                            t.wait_ge(dve.h, dve.n)
                        for e in range(EC):
                            inst = t.matmul(stat_s[:], onecol, xb(e),
                                            start=(e == 0), stop=(e == EC - 1))
                        pe.inc(inst, 1)

                    SUM_DONE = pe.n

                    @blk.vector
                    def _(v):
                        v.wait_ge(ld_c.h, CONST_LD)
                        v.wait_ge(pe.h, SUM_DONE)
                        dve.inc(v.tensor_scalar(out=m_row, in0=stat_s[:],
                                                scalar1=C_LNSC, scalar2=None,
                                                op0=ALU.mult), 1)

                    MR_DONE = dve.n

                    @blk.tensor
                    def _(t):
                        t.wait_ge(dve.h, max(MR_DONE, SQ_DONE))
                        for e in range(EC):
                            inst = t.matmul(stat_s2[:], onecol, sqt(e),
                                            start=(e == 0), stop=(e == EC - 1))
                        pe.inc(inst, 1)

                    STATS_DONE = pe.n
                    STATS_PREV = STATS_DONE

                    @blk.vector
                    def _(v):
                        v.wait_ge(pe.h, STATS_DONE)
                        dve.inc(v.tensor_scalar(out=s2_row, in0=stat_s2[:],
                                                scalar1=C_LNSC, scalar2=None,
                                                op0=ALU.mult), 1)
                        v.wait_ge(dve.h, dve.n)
                        dve.inc(v.tensor_tensor(out=msq_row, in0=m_row, in1=m_row,
                                                op=ALU.mult), 1)
                        v.wait_ge(dve.h, dve.n)
                        dve.inc(v.tensor_tensor(out=var_row, in0=s2_row, in1=msq_row,
                                                op=ALU.subtract), 1)

                    VAR_DONE = dve.n

                    @blk.scalar
                    def _(s):
                        s.wait_ge(dve.h, VAR_DONE)
                        act.inc(s.activation(out=lnt_row, in_=var_row, func=AF.Ln,
                                             bias=LN_EPS), 1)
                        s.wait_ge(act.h, act.n)
                        act.inc(s.activation(out=rstd_row, in_=lnt_row, func=AF.Exp,
                                             scale=-0.5), 1)

                    RSTD_DONE = act.n

                    @blk.vector
                    def _(v):
                        v.wait_ge(act.h, RSTD_DONE)
                        dve.inc(v.tensor_tensor(out=mrs_row, in0=m_row, in1=rstd_row,
                                                op=ALU.mult), 1)
                        v.wait_ge(dve.h, dve.n)
                        v.tensor_copy(rstd_b, rstd_row)
                        dve.inc(v.tensor_copy(mrs_b, mrs_row), 1)

                    SMALL_DONE = dve.n

                    @blk.tensor
                    def _(t):
                        t.wait_ge(dve.h, SMALL_DONE)
                        t.matmul(bc1[:], onerow, rstd_b, start=True, stop=True)
                        pe.inc(t.matmul(bc2[:], onerow, mrs_b, start=True, stop=True), 1)

                    BC_DONE = pe.n

                    @blk.vector
                    def _(v, tg=tg, xb_a=xb_a):
                        v.wait_ge(pe.h, BC_DONE)
                        v.tensor_copy(rstd_bc, bc1[:])
                        dve.inc(v.tensor_copy(mrs_bc, bc2[:]), 1)
                        v.wait_ge(dve.h, dve.n)
                        xb3 = xb_a[:].rearrange("p (e t) -> p e t", t=512)
                        sq3 = sq_a[:].rearrange("p (e t) -> p e t", t=512)
                        h3 = h_big[:].rearrange("p (e c) -> p e c", c=TPC)[
                            :, :, 512 * tg:512 * tg + 512]
                        rbc3 = rstd_bc.rearrange("p t -> p () t").to_broadcast(
                            (128, EC, 512))
                        mbc3 = mrs_bc.rearrange("p t -> p () t").to_broadcast(
                            (128, EC, 512))
                        dve.inc(v.tensor_tensor(out=sq3, in0=xb3, in1=rbc3,
                                                op=ALU.mult), 1)
                        v.wait_ge(dve.h, dve.n)
                        dve.inc(v.tensor_tensor(out=h3, in0=sq3, in1=mbc3,
                                                op=ALU.subtract), 1)
                    H_TG[tg] = dve.n
                    XB_FREE[tg % 2] = (STATS_DONE, dve.n)

            H_DONE = dve.n

            def emit_qkv_section(sec, tsb_all=None, h_gate=None, barrier=True,
                                 shared=None, S=None, pre_wld=None):
                fcs = {"v": range(32, 48), "k": range(16, 32), "q": range(0, 16)}[sec]
                if barrier:
                    full_barrier()
                with contextlib.ExitStack() as ps:
                    if shared is None:
                        wqs = [ps.enter_context(
                            nc.sbuf_tensor(f"wq{i}_{sec}", [128, 2048], BF16))
                            for i in range(2)]
                        qkp = [ps.enter_context(
                            nc.psum_tensor(f"qkp{i}_{sec}", [128, 1024], F32))
                            for i in range(2)]
                        w_free = [0, 0]
                        p_free = [0, 0]
                    else:
                        wqs, qkp = shared
                        w_free = S["w_free"]
                        p_free = S["p_free"]
                    if sec == "v":
                        v_sb = ps.enter_context(nc.sbuf_tensor("v_sb", [128, 1024], BF16))
                        vt_sb = ps.enter_context(nc.sbuf_tensor("vt_sb", [128, 1024], BF16))
                        vt_ps = ps.enter_context(nc.psum_tensor("vt_ps", [128, 512], BF16))

                    v_sb_free = 0
                    vt_ps_free = 0
                    vt_sb_free = [0, 0]

                    fill_i = 0
                    for idx, fc in enumerate(fcs):
                        wb = idx % 2
                        if pre_wld is not None and idx < len(pre_wld):
                            W_LD = pre_wld[idx]
                        else:
                            @blk.gpsimd
                            def _(g, fc=fc, wb=wb, WF=w_free[wb]):
                                if WF:
                                    g.wait_ge(pe.h, WF)
                                src = wqkvT.rearrange("(ec e) f -> e ec f", e=128)[
                                    :, :, 128 * fc:128 * fc + 128]
                                dst = wqs[wb][:].rearrange("p (ec f) -> p ec f", f=128)
                                ld_w[wb].inc(g.dma_start(out=dst, in_=src), 16)

                            W_LD = ld_w[wb].n

                        for hh in range(2):
                            r = fill_i % 2
                            fill_i += 1

                            @blk.tensor
                            def _(t, wb=wb, hh=hh, r=r, W_LD=W_LD, PF=p_free[r]):
                                t.wait_ge(ld_w[wb].h, W_LD)
                                t.wait_ge(dve.h, h_gate[hh] if h_gate else H_DONE)
                                if PF:
                                    t.wait_ge(dve.h, PF)
                                for h2 in range(2):
                                    tgc = 1024 * hh + 512 * h2
                                    for e in range(EC):
                                        inst = t.matmul(
                                            qkp[r][:, 512 * h2:512 * h2 + 512],
                                            wqs[wb][:, 128 * e:128 * e + 128],
                                            h_all[e][:, tgc:tgc + 512],
                                            start=(e == 0), stop=(e == EC - 1))
                                pe.inc(inst, 1)

                            FILL_DONE = pe.n
                            if hh == 1:
                                w_free[wb] = FILL_DONE

                            if sec == "v":
                                hd = fc - 32
                                p_pair = 16 * hh + hd
                                jj, ss = p_pair // 4, p_pair % 4
                                vbase = _kv_row(jj, ss, 1) * 1024

                                @blk.vector
                                def _(v, fc=fc, r=r, FILL_DONE=FILL_DONE,
                                      VF=v_sb_free):
                                    v.wait_ge(pe.h, FILL_DONE)
                                    if VF:
                                        v.wait_ge(pe.h, VF)
                                    dve.inc(v.tensor_scalar(
                                        out=v_sb[:], in0=qkp[r][:],
                                        scalar1=bq_sb[:, fc:fc + 1], scalar2=None,
                                        op0=ALU.add), 1)

                                EVAC = dve.n
                                p_free[r] = EVAC

                                for grp in range(2):
                                    @blk.tensor
                                    def _(t, grp=grp, EVAC=EVAC, VPF=vt_ps_free):
                                        t.wait_ge(dve.h, EVAC)
                                        if VPF:
                                            t.wait_ge(dve.h, VPF)
                                        for tb in range(4):
                                            tbg = 4 * grp + tb
                                            inst = t.transpose(
                                                vt_ps[:, 128 * tb:128 * tb + 128],
                                                v_sb[:, 128 * tbg:128 * tbg + 128],
                                                id_sb[:])
                                        pe.inc(inst, 1)

                                    TP_D = pe.n

                                    @blk.vector
                                    def _(v, grp=grp, TP_D=TP_D,
                                          VSF=vt_sb_free[grp]):
                                        v.wait_ge(pe.h, TP_D)
                                        if VSF:
                                            v.wait_ge(st_v[grp].h, VSF)
                                        dve.inc(v.tensor_copy(
                                            vt_sb[:, 512 * grp:512 * grp + 512],
                                            vt_ps[:]), 1)

                                    VT_C = dve.n
                                    vt_ps_free = VT_C

                                    @blk.scalar
                                    def _(s, grp=grp, VT_C=VT_C, vbase=vbase):
                                        s.wait_ge(dve.h, VT_C)
                                        dsta = kvc_f[
                                            vbase + 65536 * grp:
                                            vbase + 65536 * grp + 65536
                                        ].rearrange("(tb t d) -> t tb d",
                                                    tb=4, t=128, d=128)
                                        srca = vt_sb[:, 512 * grp:512 * grp + 512
                                                     ].rearrange("p (tb d) -> p tb d",
                                                                 d=128)
                                        st_v[grp].inc(s.dma_start(out=dsta, in_=srca), 16)
                                        vt_sb_free[grp] = st_v[grp].n

                                v_sb_free = pe.n
                            else:
                                ci = fc - (16 if sec == "k" else 0)

                                @blk.vector
                                def _(v, fc=fc, ci=ci, hh=hh, r=r, FILL_DONE=FILL_DONE):
                                    v.wait_ge(pe.h, FILL_DONE)
                                    dve.inc(v.tensor_scalar(
                                        out=tsb_all[ci][:, 1024 * hh:1024 * hh + 1024],
                                        in0=qkp[r][:],
                                        scalar1=bq_sb[:, fc:fc + 1], scalar2=None,
                                        op0=ALU.add), 1)

                                p_free[r] = dve.n

            def emit_rope_norm(sec, tsb_all, R):
                typ = 0 if sec == "q" else 1
                tg_order = [0, 2, 1, 3] if sec == "q" else [0, 1, 2, 3]
                EVAC_ALL = dve.n
                qrp, aux = R["qrp"], R["aux"]
                nsq_ps = aux[0:1, 0:512]
                qr_a, sq2, nr, nrb = R["qr_a"], R["sq2"], R["nr"], R["nrb"]
                rbc_sb, qh = R["rbc_sb"], R["qh"]
                QR_WAR, NSQ_WAR, qh_st = R["QR_WAR"], R["NSQ_WAR"], R["qh_st"]
                ln_t = nr[:, 512:1024]
                n_row = nr[:, 1024:1536]
                r_row = nr[:, 1536:2048]

                def qr(c):
                    return qr_a[:, 512 * c:512 * c + 512]

                for tg_i, tg in enumerate(tg_order):
                    b, hf = tg // 2, tg % 2
                    tgc = 512 * tg
                    act_base = act.n
                    dve_base = dve.n
                    EV = {c: act_base + c + 1 for c in range(EC)}
                    SQ = {c: dve_base + c + 1 for c in range(EC)}
                    NRB_DONE = dve_base + 16 + 4
                    rope_done = {}
                    nsq_done = {}
                    PRE_NW = dict(NSQ_WAR)

                    @blk.tensor
                    def _(t, tg=tg, tgc=tgc, tg_i=tg_i, EV=EV, SQ=SQ,
                          NRB_DONE=NRB_DONE, QW=dict(QR_WAR)):
                        def nsq(c):
                            t.wait_ge(dve.h, SQ[c])
                            inst = t.matmul(nsq_ps, onecol,
                                            sq2[:, 512 * (c % 3):512 * (c % 3) + 512],
                                            start=(c == 0), stop=(c == EC - 1))
                            pe.inc(inst, 1)
                            nsq_done[c] = pe.n
                            NSQ_WAR[c % 3] = pe.n

                        for c in range(EC):
                            src1 = 2 * c if c < 8 else 2 * (c - 8)
                            src2 = src1 + 1
                            rp = c % 3
                            if tg_i == 0 and c == 0:
                                t.wait_ge(ld_c.h, ROPEM_LD)
                                t.wait_ge(dve.h, EVAC_ALL)
                            if QW.get(rp):
                                t.wait_ge(act.h, QW[rp])
                            t.matmul(qrp[rp][:], rm(3 * c),
                                     tsb_all[c][:, tgc:tgc + 512],
                                     start=True, stop=False)
                            t.matmul(qrp[rp][:], rm(3 * c + 1),
                                     tsb_all[src1][:, tgc:tgc + 512],
                                     start=False, stop=False)
                            inst = t.matmul(qrp[rp][:], rm(3 * c + 2),
                                            tsb_all[src2][:, tgc:tgc + 512],
                                            start=False, stop=True)
                            pe.inc(inst, 1)
                            rope_done[c] = pe.n
                            QW[rp] = EV[c]
                            QR_WAR[rp] = EV[c]
                            if c >= 2:
                                nsq(c - 2)
                        nsq(EC - 2)
                        nsq(EC - 1)
                        t.wait_ge(dve.h, NRB_DONE)
                        pe.inc(t.matmul(aux[:, :], onerow, nrb[:],
                                        start=True, stop=True), 1)

                    RBC_DONE = pe.n
                    NSQ_ALL = nsq_done[EC - 1]

                    @blk.scalar
                    def _(s, NSQ_ALL=NSQ_ALL):
                        for c in range(EC):
                            s.wait_ge(pe.h, rope_done[c])
                            act.inc(s.activation(out=qr(c), in_=qrp[c % 3][:],
                                                 func=AF.Identity), 1)
                        s.wait_ge(pe.h, NSQ_ALL)
                        act.inc(s.activation(out=ln_t, in_=nsq_ps, func=AF.Ln), 1)
                        s.wait_ge(act.h, act.n)
                        act.inc(s.activation(out=n_row, in_=ln_t, func=AF.Exp,
                                             scale=0.5), 1)

                    N_DONE = act.n
                    assert N_DONE == act_base + 18

                    @blk.vector
                    def _(v, N_DONE=N_DONE, EV=EV, PRE_NW=PRE_NW):
                        for c in range(EC):
                            v.wait_ge(act.h, EV[c])
                            if c >= 3:
                                v.wait_ge(pe.h, nsq_done[c - 3])
                            elif PRE_NW.get(c % 3):
                                v.wait_ge(pe.h, PRE_NW[c % 3])
                            dve.inc(v.tensor_tensor(
                                out=sq2[:, 512 * (c % 3):512 * (c % 3) + 512],
                                in0=qr(c), in1=qr(c), op=ALU.mult), 1)
                        v.wait_ge(act.h, N_DONE)
                        dve.inc(v.tensor_scalar(out=n_row, in0=n_row,
                                                scalar1=NORM_EPS, scalar2=None,
                                                op0=ALU.max), 1)
                        v.wait_ge(dve.h, dve.n)
                        dve.inc(v.reciprocal(out=r_row, in_=n_row), 1)
                        v.wait_ge(dve.h, dve.n)
                        dve.inc(v.tensor_scalar(out=r_row, in0=r_row,
                                                scalar1=float(SCALE), scalar2=None,
                                                op0=ALU.mult), 1)
                        v.wait_ge(dve.h, dve.n)
                        dve.inc(v.tensor_copy(nrb[:], r_row), 1)

                    assert dve.n == NRB_DONE, (dve.n, NRB_DONE)

                    @blk.vector
                    def _(v, RBC_DONE=RBC_DONE, QS0=qh_st[0], QS1=qh_st[1]):
                        v.wait_ge(pe.h, RBC_DONE)
                        if QS0:
                            v.wait_ge(st_m[0].h, QS0)
                        if QS1:
                            v.wait_ge(st_m[1].h, QS1)
                        dve.inc(v.tensor_copy(rbc_sb[:], aux[:, :]), 1)
                        for c in range(EC):
                            hp = c % 4
                            dve.inc(v.tensor_tensor(
                                out=qh[:, 512 * hp:512 * hp + 512],
                                in0=qr(c), in1=rbc_sb[:], op=ALU.mult), 1)

                    QH_BASE = dve.n - EC

                    @blk.sync
                    def _(sy, b=b, hf=hf, QH_BASE=QH_BASE):
                        for c in range(EC):
                            hp = c % 4
                            sy.wait_ge(dve.h, QH_BASE + c + 1)
                            p_pair = 16 * b + c
                            jj, ss = p_pair // 4, p_pair % 4
                            if typ == 0:
                                dst = qch[hf][_q_row(jj, ss):_q_row(jj, ss) + 128, :]
                            else:
                                row = _kv_row(jj, ss, 0)
                                dst = kvc[row:row + 128, 512 * hf:512 * hf + 512]
                            st_m[hp % 2].inc(sy.dma_start(
                                out=dst,
                                in_=qh[:, 512 * hp:512 * hp + 512]), 16)

                    qh_st[0], qh_st[1] = st_m[0].n, st_m[1].n
                    if sec == "q" and tg_i in (1, 3):
                        hf_done = 0 if tg_i == 1 else 1
                        ccq = cc3 if hf_done == 0 else cc5
                        QM0, QM1 = st_m[0].n, st_m[1].n

                        @blk.gpsimd
                        def _(gp, hf_done=hf_done, ccq=ccq, QM0=QM0, QM1=QM1):
                            gp.wait_ge(st_m[0].h, QM0)
                            gp.wait_ge(st_m[1].h, QM1)
                            with allow_ici_all_to_all():
                                ccq.inc_default(gp.collective_compute(
                                    "AllToAll", ALU.bypass,
                                    replica_groups=[list(range(NCORES))],
                                    ins=[qch[hf_done].opt()],
                                    outs=[qgh[hf_done].opt()]))

            emit_qkv_section("v", h_gate=(H_TG[1], H_TG[3]), barrier=False)
            pl.close()
            full_barrier()
            with contextlib.ExitStack() as pa2:
                tsb_big = pa2.enter_context(
                    nc.sbuf_tensor("tsb_big", [128, EC * 2048], BF16))
                tsb_sh = [tsb_big[:, 2048 * i:2048 * i + 2048] for i in range(EC)]
                wq_sh = [pa2.enter_context(
                    nc.sbuf_tensor(f"wqsh{i}", [128, 2048], BF16)) for i in range(2)]
                qkp_sh = [pa2.enter_context(
                    nc.psum_tensor(f"qkpsh{i}", [128, 1024], F32)) for i in range(2)]
                R = {
                    "qr_a": pa2.enter_context(
                        nc.sbuf_tensor("qr_sh", [128, 512 * 16], BF16)),
                    "sq2": pa2.enter_context(
                        nc.sbuf_tensor("sq2_sh", [128, 1536], BF16)),
                    "nr": pa2.enter_context(nc.sbuf_tensor("nr_sh", [1, 2048], F32)),
                    "nrb": pa2.enter_context(nc.sbuf_tensor("nrb_sh", [1, 512], BF16)),
                    "rbc_sb": pa2.enter_context(
                        nc.sbuf_tensor("rbc_sh", [128, 512], BF16)),
                    "qh": pa2.enter_context(nc.sbuf_tensor("qh_sh", [128, 2048], BF16)),
                    "qrp": [pa2.enter_context(
                        nc.psum_tensor(f"qrpsh{i}", [128, 512], F32)) for i in range(3)],
                    "aux": pa2.enter_context(nc.psum_tensor("aux_sh", [128, 512], F32)),
                    "QR_WAR": {}, "NSQ_WAR": {}, "qh_st": [0, 0],
                }
                S = {"w_free": [0, 0], "p_free": [0, 0]}
                emit_qkv_section("k", tsb_all=tsb_sh, barrier=False,
                                 shared=(wq_sh, qkp_sh), S=S)
                emit_rope_norm("k", tsb_sh, R)

                pre_wld = []
                for idx in range(2):
                    @blk.gpsimd
                    def _(g, fc=idx, wb=idx, WF=S["w_free"][idx]):
                        if WF:
                            g.wait_ge(pe.h, WF)
                        src = wqkvT.rearrange("(ec e) f -> e ec f", e=128)[
                            :, :, 128 * fc:128 * fc + 128]
                        dst = wq_sh[wb][:].rearrange("p (ec f) -> p ec f", f=128)
                        ld_w[wb].inc(g.dma_start(out=dst, in_=src), 16)
                    pre_wld.append(ld_w[idx].n)
                    S["w_free"][idx] = 0

                KV_M0, KV_M1 = st_m[0].n, st_m[1].n
                KV_V0, KV_V1 = st_v[0].n, st_v[1].n

                @blk.gpsimd
                def _(g):
                    g.wait_ge(st_m[0].h, KV_M0)
                    g.wait_ge(st_m[1].h, KV_M1)
                    g.wait_ge(st_v[0].h, KV_V0)
                    g.wait_ge(st_v[1].h, KV_V1)
                    with allow_ici_all_to_all():
                        cc1.inc_default(g.collective_compute(
                            "AllToAll", ALU.bypass,
                            replica_groups=[list(range(NCORES))],
                            ins=[kvc.opt()], outs=[kvg.opt()]))

                emit_qkv_section("q", tsb_all=tsb_sh, barrier=False,
                                 shared=(wq_sh, qkp_sh), S=S, pre_wld=pre_wld)
                emit_rope_norm("q", tsb_sh, R)



        full_barrier()
        with contextlib.ExitStack() as pb_:
            kc = [pb_.enter_context(nc.sbuf_tensor(f"kc{i}", [128, 8192], BF16))
                  for i in range(2)]
            vc = [pb_.enter_context(nc.sbuf_tensor(f"vc{i}", [128, 8192], BF16))
                  for i in range(2)]
            qb2 = pb_.enter_context(nc.sbuf_tensor("qb2", [128, 1024], BF16))
            pbuf = [pb_.enter_context(nc.sbuf_tensor(f"pb{i}", [128, 1024], BF16))
                    for i in range(4)]
            pmb = [pb_.enter_context(nc.sbuf_tensor(f"pm{i}", [128, 1024], BF16))
                   for i in range(2)]
            asum = [pb_.enter_context(nc.sbuf_tensor(f"asum{i}", [128, 1024], F32))
                    for i in range(2)]
            ptmp = pb_.enter_context(nc.sbuf_tensor("ptmp", [128, 1024], BF16))
            mask_sb = pb_.enter_context(nc.sbuf_tensor("mask_sb", [128, 2048], BF16))
            asum_bf = pb_.enter_context(nc.sbuf_tensor("asum_bf", [128, 1024], BF16))
            ep_row = pb_.enter_context(nc.sbuf_tensor("ep_row", [1, 1024], F32))
            ep_rowb = pb_.enter_context(nc.sbuf_tensor("ep_rowb", [1, 512], BF16))
            rbcB = pb_.enter_context(nc.sbuf_tensor("rbcB", [128, 512], BF16))
            ao = [pb_.enter_context(nc.sbuf_tensor(f"ao{i}", [128, 512], BF16))
                  for i in range(2)]
            scp = [pb_.enter_context(nc.psum_tensor(f"scp{i}", [128, 1024], F32))
                   for i in range(2)]
            accat = [pb_.enter_context(nc.psum_tensor(f"accat{i}", [128, 512], F32))
                     for i in range(2)]
            accA = [pb_.enter_context(nc.psum_tensor(f"accA{i}", [128, 512], F32))
                    for i in range(2)]

            @blk.sync
            def _(sy):
                src = masksd.rearrange("m p c -> p m c")
                dst = mask_sb[:].rearrange("p (m c) -> p m c", c=1024)
                ld_c.inc(sy.dma_start(out=dst, in_=src), 16)

            MASK_LD = ld_c.n

            sc_free = [0, 0]
            pb_war = [None, None, None, None]
            pm_free = [0, 0]
            qb_ready = [0, 0]
            qb_free = [0, 0]
            ao_free = [0, 0]
            epi_done = [0, 0]
            kv_ready = [0, 0]
            pair_pe_end = [0, 0, 0, 0]

            def emit_kv_loads(s):
                bufp = s % 2

                @blk.gpsimd
                def _(g, s=s, bufp=bufp):
                    if s == 0:
                        g.wait_ge(cc1.h, 1)
                    if s >= 2:
                        g.wait_ge(pe.h, pair_pe_end[s - 2])
                    for gch in range(16):
                        r, hf = gch // 2, gch % 2
                        krow = _kv_row(r, s, 0)
                        ld_kv[bufp].inc(g.dma_start(
                            out=kc[bufp][:, 512 * gch:512 * gch + 512],
                            in_=kvg[krow:krow + 128, 512 * hf:512 * hf + 512]), 16)
                        vbase = _kv_row(r, s, 1) * 1024 + 65536 * hf
                        src = kvg_f[vbase:vbase + 65536].rearrange(
                            "(u t d) -> t u d", u=4, t=128, d=128)
                        dst = vc[bufp][:, 512 * gch:512 * gch + 512].rearrange(
                            "p (u d) -> p u d", d=128)
                        ld_kv[bufp].inc(g.dma_start(out=dst, in_=src), 16)
                kv_ready[bufp] = ld_kv[bufp].n

            emit_kv_loads(0)
            emit_kv_loads(1)

            sched = [(s, g) for s in range(4) for g in range(16)]

            def emit_qb_load(i):
                s, g = sched[i]
                par = i % 2
                r, hf = g // 2, g % 2

                @blk.sync
                def _(sy, s=s, g=g, par=par, r=r, hf=hf, i=i, QF=qb_free[par]):
                    if i == 0:
                        sy.wait_ge(cc3.h, 1)
                    if i == 1:
                        sy.wait_ge(cc5.h, 1)
                    if QF:
                        sy.wait_ge(pe.h, QF)
                    qrow = _q_row(r, s)
                    ld_q[par].inc(sy.dma_start(
                        out=qb2[:, 512 * par:512 * par + 512],
                        in_=qgh[hf][qrow:qrow + 128, :]), 16)
                qb_ready[par] = ld_q[par].n

            emit_qb_load(0)

            pend = None
            epi_mm_prev = {}

            def emit_pend_store(pend_, EPI_F_val):
                @blk.sync
                def _(sy, pend_=pend_, EPI_F_val=EPI_F_val):
                    sy.wait_ge(dve.h, EPI_F_val)
                    dstT = aoc0 if pend_["s"] < 2 else aoc1
                    row = 256 * pend_["r"] + 128 * (pend_["s"] % 2)
                    st_m[pend_["gpar"]].inc(sy.dma_start(
                        out=dstT[row:row + 128,
                                 512 * pend_["hf"]:512 * pend_["hf"] + 512],
                        in_=ao[pend_["gpar"]][:]), 16)
                    ao_free[pend_["gpar"]] = st_m[pend_["gpar"]].n
                if pend_["s"] == 1 and pend_["g"] == 15:
                    A0_M0, A0_M1 = st_m[0].n, st_m[1].n

                    @blk.gpsimd
                    def _(gp, A0_M0=A0_M0, A0_M1=A0_M1):
                        gp.wait_ge(st_m[0].h, A0_M0)
                        gp.wait_ge(st_m[1].h, A0_M1)
                        with allow_ici_all_to_all():
                            cc2.inc_default(gp.collective_compute(
                                "AllToAll", ALU.bypass,
                                replica_groups=[list(range(NCORES))],
                                ins=[aoc0.opt()], outs=[aog0.opt()]))

            act_sim = act.n
            sc_defer = []
            dve_sim = dve.n
            dv_defer = []
            hoist_rec = None

            for i, (s, g) in enumerate(sched):
                bufp = s % 2
                gpar = i % 2
                r, hf = g // 2, g % 2
                if i + 1 < len(sched):
                    emit_qb_load(i + 1)

                npair = 2 * g + 2
                act_base = act_sim
                dve_base = dve_sim
                nrec = 2 if pend else 0
                exp_done = {tp: act_base + tp + 1 + (nrec if tp >= 2 else 0)
                            for tp in range(npair)}
                REC_prev = act_base + nrec + 2
                act_sim += nrec + npair
                nepi = 3 if pend else 0
                EPI_A_prev = dve_base + 1
                EPI_F_prev = dve_base + 3
                base_d = dve_base + nepi
                presum_done = {}
                for te in range(0, 2 * g, 2):
                    presum_done[te] = base_d + te + 1
                    presum_done[te + 1] = base_d + te + 1
                mask_done = {0: base_d + 2 * g + 1, 1: base_d + 2 * g + 2}
                presum_done[2 * g] = base_d + 2 * g + 3
                presum_done[2 * g + 1] = base_d + 2 * g + 3
                DVE_END = base_d + 2 * g + 4
                dve_sim = DVE_END

                scores_done = {}
                iav_done = {}
                exp_war = {}
                epi_mm = {}
                ED_prev = epi_done[gpar]
                PM_prev = list(pm_free)
                QR = qb_ready[gpar]
                KR = kv_ready[bufp]

                hoisted = hoist_rec
                hoist_rec = None
                if hoisted is not None:
                    scores_done[0], exp_war[0] = hoisted[0]
                    scores_done[1], exp_war[1] = hoisted[1]
                have_next = i + 1 < len(sched)
                if have_next:
                    ns_, ng_ = sched[i + 1]
                    nbufp, ngpar = ns_ % 2, (i + 1) % 2
                    NQR = qb_ready[ngpar]
                    NKR = kv_ready[nbufp]
                    exp_done_next = {0: act_sim + 1, 1: act_sim + 2}
                    nmask_done = {0: dve_sim + 3 + 2 * ng_ + 1,
                                  1: dve_sim + 3 + 2 * ng_ + 2}
                    hoist_rec = {}

                @blk.tensor
                def _(t, s=s, g=g, gpar=gpar, bufp=bufp, npair=npair,
                      ED_prev=ED_prev, QR=QR, KR=KR, pend=pend,
                      EPI_A_prev=EPI_A_prev, REC_prev=REC_prev,
                      hoisted=hoisted, hoist_rec=hoist_rec,
                      have_next=have_next):
                    def iav(tp):
                        p2 = tp % 2
                        p4 = tp % 4
                        diag = tp >= 2 * g
                        srcb = pmb[p2] if diag else pbuf[p4]
                        if diag:
                            t.wait_ge(dve.h, mask_done[tp - 2 * g])
                        else:
                            t.wait_ge(act.h, exp_done[tp])
                        if tp == 0 and ED_prev:
                            t.wait_ge(dve.h, ED_prev)
                        j0, j1 = 2 * tp, 2 * tp + 1
                        t.matmul(accat[gpar][:],
                                 vc[bufp][:, 128 * j0:128 * j0 + 128],
                                 srcb[:, 0:512], start=(tp == 0), stop=False)
                        inst = t.matmul(accat[gpar][:],
                                        vc[bufp][:, 128 * j1:128 * j1 + 128],
                                        srcb[:, 512:1024],
                                        start=False, stop=(tp == npair - 1))
                        pe.inc(inst, 1)
                        iav_done[tp] = pe.n
                        if diag:
                            pm_free[p2] = pe.n

                    def sco(tp):
                        p2 = tp % 2
                        if sc_free[p2]:
                            t.wait_ge(act.h, sc_free[p2])
                        if tp == 0:
                            t.wait_ge(ld_q[gpar].h, QR)
                            t.wait_ge(ld_kv[bufp].h, KR)
                        j0, j1 = 2 * tp, 2 * tp + 1
                        t.matmul(scp[p2][:, 0:512],
                                 kc[bufp][:, 128 * j0:128 * j0 + 128],
                                 qb2[:, 512 * gpar:512 * gpar + 512],
                                 start=True, stop=True)
                        inst = t.matmul(scp[p2][:, 512:1024],
                                        kc[bufp][:, 128 * j1:128 * j1 + 128],
                                        qb2[:, 512 * gpar:512 * gpar + 512],
                                        start=True, stop=True)
                        pe.inc(inst, 1)
                        scores_done[tp] = pe.n
                        diag = tp >= 2 * g
                        exp_war[tp] = pb_war[tp % 4]
                        if diag:
                            pb_war[tp % 4] = [(dve, mask_done[tp - 2 * g])]
                        sc_free[p2] = exp_done[tp]

                    def epi_s():
                        t.wait_ge(dve.h, EPI_A_prev)
                        t.matmul(accA[pend["gpar"]][0:1, 0:512],
                                 onecol, asum_bf[:, 0:512],
                                 start=True, stop=False)
                        pe.inc(t.matmul(accA[pend["gpar"]][0:1, 0:512],
                                        onecol, asum_bf[:, 512:1024],
                                        start=False, stop=True), 1)
                        epi_mm["S"] = pe.n

                    def sco_next(tp):
                        p2 = tp % 2
                        if sc_free[p2]:
                            t.wait_ge(act.h, sc_free[p2])
                        if tp == 0:
                            t.wait_ge(ld_q[ngpar].h, NQR)
                            t.wait_ge(ld_kv[nbufp].h, NKR)
                        j0, j1 = 2 * tp, 2 * tp + 1
                        t.matmul(scp[p2][:, 0:512],
                                 kc[nbufp][:, 128 * j0:128 * j0 + 128],
                                 qb2[:, 512 * ngpar:512 * ngpar + 512],
                                 start=True, stop=True)
                        inst = t.matmul(scp[p2][:, 512:1024],
                                        kc[nbufp][:, 128 * j1:128 * j1 + 128],
                                        qb2[:, 512 * ngpar:512 * ngpar + 512],
                                        start=True, stop=True)
                        pe.inc(inst, 1)
                        hoist_rec[tp] = (pe.n, pb_war[tp % 2])
                        if ng_ == 0:
                            pb_war[tp % 2] = [(dve, nmask_done[tp])]
                        sc_free[p2] = exp_done_next[tp]

                    if hoisted is None:
                        sco(0)
                        sco(1)
                    if pend is not None and npair == 2:
                        epi_s()
                    for tp in range(npair):
                        if tp + 2 < npair:
                            sco(tp + 2)
                        if pend is not None and npair > 2 and tp == 0:
                            epi_s()
                        if pend is not None and tp == (0 if npair == 2 else 1):
                            t.wait_ge(act.h, REC_prev)
                            pe.inc(t.matmul(accA[pend["gpar"]][:, :],
                                            onerow, ep_rowb[:],
                                            start=True, stop=True), 1)
                            epi_mm["B"] = pe.n
                        if tp == 2 * g and have_next:
                            sco_next(0)
                            sco_next(1)
                        iav(tp)
                        if tp < 2 * g:
                            pb_war[tp % 4] = [(pe, iav_done[tp]),
                                              (dve, presum_done[tp])]

                LAST_IAV = iav_done[npair - 1]
                qb_free[gpar] = pe.n

                sc_defer.append(dict(npair=npair,
                                     pend_gpar=(pend["gpar"] if pend else None),
                                     epi_mm=epi_mm, scores_done=scores_done,
                                     exp_war=exp_war))

                dv_defer.append(dict(
                    g=g, gpar=gpar, PM_prev=PM_prev, exp_done=exp_done,
                    pend_gpar=(pend["gpar"] if pend else None),
                    pend_LAST_IAV=(pend["LAST_IAV"] if pend else 0),
                    AF_=(ao_free[pend["gpar"]] if pend else 0),
                    epi_mm=epi_mm, DVE_END=DVE_END))
                if pend is not None:
                    epi_done[pend["gpar"]] = EPI_F_prev
                    emit_pend_store(pend, EPI_F_prev)
                pend = {"gpar": gpar, "LAST_IAV": LAST_IAV,
                        "s": s, "g": g, "r": r, "hf": hf}

                if g == 15:
                    pair_pe_end[s] = pe.n
                    if s + 2 < 4:
                        emit_kv_loads(s + 2)
                    vrecs = dv_defer
                    dv_defer = []

                    @blk.vector
                    def _(v, vrecs=vrecs):
                        for rc in vrecs:
                            rg, rgpar = rc["g"], rc["gpar"]
                            ed = rc["exp_done"]
                            if rc["pend_gpar"] is not None:
                                pg = rc["pend_gpar"]
                                v.wait_ge(pe.h, rc["pend_LAST_IAV"])
                                dve.inc(v.tensor_copy(asum_bf[:],
                                                      asum[pg][:]), 1)
                                v.wait_ge(pe.h, rc["epi_mm"]["B"])
                                dve.inc(v.tensor_copy(rbcB[:],
                                                      accA[pg][:, :]), 1)
                                v.wait_ge(dve.h, dve.n)
                                if rc["AF_"]:
                                    v.wait_ge(st_m[pg].h, rc["AF_"])
                                dve.inc(v.tensor_tensor(out=ao[pg][:],
                                                        in0=accat[pg][:],
                                                        in1=rbcB[:],
                                                        op=ALU.mult), 1)
                            for te in range(0, 2 * rg, 2):
                                v.wait_ge(act.h, ed[te + 1])
                                dve.inc(v.tensor_tensor(
                                    out=ptmp[:], in0=pbuf[te % 4][:],
                                    in1=pbuf[(te + 1) % 4][:],
                                    op=ALU.add), 1)
                                if te == 0:
                                    dve.inc(v.tensor_copy(asum[rgpar][:],
                                                          ptmp[:]), 1)
                                else:
                                    dve.inc(v.tensor_tensor(
                                        out=asum[rgpar][:],
                                        in0=asum[rgpar][:],
                                        in1=ptmp[:], op=ALU.add), 1)
                            for mi in range(2):
                                tp = 2 * rg + mi
                                v.wait_ge(act.h, ed[tp])
                                v.wait_ge(ld_c.h, MASK_LD)
                                if rc["PM_prev"][tp % 2]:
                                    v.wait_ge(pe.h, rc["PM_prev"][tp % 2])
                                dve.inc(v.tensor_tensor(
                                    out=pmb[tp % 2][:], in0=pbuf[tp % 4][:],
                                    in1=mask_sb[:, 1024 * mi:1024 * mi + 1024],
                                    op=ALU.mult), 1)
                            dve.inc(v.tensor_tensor(out=ptmp[:], in0=pmb[0][:],
                                                    in1=pmb[1][:],
                                                    op=ALU.add), 1)
                            if rg == 0:
                                dve.inc(v.tensor_copy(asum[rgpar][:],
                                                      ptmp[:]), 1)
                            else:
                                dve.inc(v.tensor_tensor(out=asum[rgpar][:],
                                                        in0=asum[rgpar][:],
                                                        in1=ptmp[:],
                                                        op=ALU.add), 1)
                            assert dve.n == rc["DVE_END"], (dve.n, rc)

                    recs = sc_defer
                    sc_defer = []

                    @blk.scalar
                    def _(sca, recs=recs):
                        for rc in recs:
                            def one_exp(tp):
                                p2 = tp % 2
                                sca.wait_ge(pe.h, rc["scores_done"][tp])
                                if rc["exp_war"][tp]:
                                    for wsem, wn in rc["exp_war"][tp]:
                                        if wn:
                                            sca.wait_ge(wsem.h, wn)
                                act.inc(sca.activation(out=pbuf[tp % 4][:],
                                                       in_=scp[p2][:],
                                                       func=AF.Exp), 1)

                            one_exp(0)
                            one_exp(1)
                            if rc["pend_gpar"] is not None:
                                sca.wait_ge(pe.h, rc["epi_mm"]["S"])
                                act.inc(sca.activation(
                                    out=ep_row[:, 0:512],
                                    in_=accA[rc["pend_gpar"]][0:1, 0:512],
                                    func=AF.Ln), 1)
                                sca.wait_ge(act.h, act.n)
                                act.inc(sca.activation(
                                    out=ep_rowb[:], in_=ep_row[:, 0:512],
                                    func=AF.Exp, scale=-1.0), 1)
                            for tp in range(2, rc["npair"]):
                                one_exp(tp)

                    assert act.n == act_sim, (act.n, act_sim)

            dve_base = dve.n
            EPI_A_f, EPI_F_f = dve_base + 1, dve_base + 3
            REC_f = act.n + 2
            epi_mm = {}

            @blk.tensor
            def _(t, pend=pend):
                t.wait_ge(dve.h, EPI_A_f)
                t.matmul(accA[pend["gpar"]][0:1, 0:512], onecol,
                         asum_bf[:, 0:512], start=True, stop=False)
                pe.inc(t.matmul(accA[pend["gpar"]][0:1, 0:512], onecol,
                                asum_bf[:, 512:1024], start=False, stop=True), 1)
                epi_mm["S"] = pe.n
                t.wait_ge(act.h, REC_f)
                pe.inc(t.matmul(accA[pend["gpar"]][:, :], onerow, ep_rowb[:],
                                start=True, stop=True), 1)
                epi_mm["B"] = pe.n

            @blk.scalar
            def _(sca, pend=pend):
                sca.wait_ge(pe.h, epi_mm["S"])
                act.inc(sca.activation(out=ep_row[:, 0:512],
                                       in_=accA[pend["gpar"]][0:1, 0:512],
                                       func=AF.Ln), 1)
                sca.wait_ge(act.h, act.n)
                act.inc(sca.activation(out=ep_rowb[:], in_=ep_row[:, 0:512],
                                       func=AF.Exp, scale=-1.0), 1)

            assert act.n == REC_f

            @blk.vector
            def _(v, pend=pend, AF_=(ao_free[pend["gpar"]] if pend else 0)):
                v.wait_ge(pe.h, pend["LAST_IAV"])
                dve.inc(v.tensor_copy(asum_bf[:], asum[pend["gpar"]][:]), 1)
                v.wait_ge(pe.h, epi_mm["B"])
                dve.inc(v.tensor_copy(rbcB[:], accA[pend["gpar"]][:, :]), 1)
                v.wait_ge(dve.h, dve.n)
                if AF_:
                    v.wait_ge(st_m[pend["gpar"]].h, AF_)
                dve.inc(v.tensor_tensor(out=ao[pend["gpar"]][:],
                                        in0=accat[pend["gpar"]][:],
                                        in1=rbcB[:], op=ALU.mult), 1)

            emit_pend_store(pend, EPI_F_f)

        AOC_M0, AOC_M1 = st_m[0].n, st_m[1].n

        @blk.gpsimd
        def _(g):
            g.wait_ge(st_m[0].h, AOC_M0)
            g.wait_ge(st_m[1].h, AOC_M1)
            with allow_ici_all_to_all():
                cc4.inc_default(g.collective_compute(
                    "AllToAll", ALU.bypass,
                    replica_groups=[list(range(NCORES))],
                    ins=[aoc1.opt()], outs=[aog1.opt()]))

        full_barrier()
        with contextlib.ExitStack() as pc:
            wb2 = [pc.enter_context(nc.sbuf_tensor(f"wb2{i}", [128, 2048], BF16))
                   for i in range(2)]
            wb2e = [pc.enter_context(nc.sbuf_tensor(f"wb2e{i}", [128, 2048], BF16))
                    for i in range(2)]

            with contextlib.ExitStack() as pc1:
                yb_big = pc1.enter_context(nc.sbuf_tensor("yb_big", [128, EC * 2048], BF16))
                yb_all = [yb_big[:, 2048 * i:2048 * i + 2048] for i in range(EC)]
                pat = pc1.enter_context(contextlib.ExitStack())
                at_big = pat.enter_context(
                    nc.sbuf_tensor("at_big", [128, EC * 2048], BF16))
                at_all = [at_big[:, 2048 * i:2048 * i + 2048] for i in range(EC)]

                @blk.gpsimd
                def _(g):
                    g.wait_ge(cc2.h, 1)
                    for ecc in range(EC):
                        if ecc % 4 >= 2:
                            continue
                        for b in range(2):
                            p = 16 * b + ecc
                            rr, sl = p // 4, p % 4
                            row = 256 * rr + 128 * (sl % 2)
                            ld_g.inc(g.dma_start(
                                out=at_all[ecc][:, 1024 * b:1024 * b + 1024],
                                in_=aog0[row:row + 128, :]), 16)

                ATTN_LD = ld_g.n

                @blk.sync
                def _(sy):
                    sy.wait_ge(cc4.h, 1)
                    for ecc in range(EC):
                        if ecc % 4 < 2:
                            continue
                        for b in range(2):
                            p = 16 * b + ecc
                            rr, sl = p // 4, p % 4
                            row = 256 * rr + 128 * (sl % 2)
                            ld_g2.inc(sy.dma_start(
                                out=at_all[ecc][:, 1024 * b:1024 * b + 1024],
                                in_=aog1[row:row + 128, :]), 16)

                ATTN_LD1 = ld_g2.n

                with contextlib.ExitStack() as pw:
                    xf = [pw.enter_context(nc.sbuf_tensor(f"xf{i}", [128, 1024], F32))
                          for i in range(2)]
                    t1 = [pw.enter_context(nc.sbuf_tensor(f"t1{i}", [128, 1024], F32))
                          for i in range(2)]
                    ysb = [pw.enter_context(nc.sbuf_tensor(f"ysb{i}", [128, 1024], F32))
                           for i in range(2)]
                    sqc = pw.enter_context(nc.sbuf_tensor("sqc", [128, 8192], BF16))
                    lnrow2 = pw.enter_context(nc.sbuf_tensor("lnrow2", [1, 512 * 7], F32))
                    lnrow2b = pw.enter_context(nc.sbuf_tensor("lnrow2b", [1, 1024], BF16))
                    ln2bc = pw.enter_context(nc.sbuf_tensor("ln2bc", [128, 2048], BF16))
                    wop = [pw.enter_context(nc.psum_tensor(f"wop{i}", [128, 1024], F32))
                           for i in range(2)]
                    st2_s = pw.enter_context(nc.psum_tensor("st2_s", [1, 512], F32))
                    st2_s2 = pw.enter_context(nc.psum_tensor("st2_s2", [1, 512], F32))
                    bc1c = pw.enter_context(nc.psum_tensor("bc1c", [128, 512], F32))
                    bc2c = pw.enter_context(nc.psum_tensor("bc2c", [128, 512], F32))

                    m_row = lnrow2[:, 0:512]
                    s2_row = lnrow2[:, 512:1024]
                    msq_row = lnrow2[:, 1024:1536]
                    var_row = lnrow2[:, 1536:2048]
                    lnt_row = lnrow2[:, 2048:2560]
                    rstd_row = lnrow2[:, 2560:3072]
                    mrs_row = lnrow2[:, 3072:3584]
                    rstd_b = lnrow2b[:, 0:512]
                    mrs_b = lnrow2b[:, 512:1024]

                    def rstd_bc2(tg):
                        return ln2bc[:, 1024 * (tg % 2):1024 * (tg % 2) + 512]

                    def mrs_bc2(tg):
                        return ln2bc[:, 1024 * (tg % 2) + 512:
                                     1024 * (tg % 2) + 1024]

                    sq_mm = [0]
                    ln2_state = {}

                    def emit_ln2a(tg):
                        tgc = 512 * tg

                        @blk.vector
                        def _(v, tgc=tgc, SQMM=sq_mm[0]):
                            if SQMM:
                                v.wait_ge(pe.h, SQMM)
                            yb3 = yb_big[:].rearrange("p (e c) -> p e c", c=2048)[
                                :, :, tgc:tgc + 512]
                            sq3 = sqc[:].rearrange("p (e t) -> p e t", t=512)
                            dve.inc(v.tensor_tensor(out=sq3, in0=yb3, in1=yb3,
                                                    op=ALU.mult), 1)

                        SQ_D = dve.n

                        @blk.tensor
                        def _(t, tgc=tgc, SQ_D=SQ_D):
                            t.wait_ge(dve.h, SQ_D)
                            for fc2 in range(EC):
                                t.matmul(st2_s[:], onecol,
                                         yb_all[fc2][:, tgc:tgc + 512],
                                         start=(fc2 == 0), stop=(fc2 == EC - 1))
                            for fc2 in range(EC):
                                inst = t.matmul(
                                    st2_s2[:], onecol,
                                    sqc[:, 512 * fc2:512 * fc2 + 512],
                                    start=(fc2 == 0), stop=(fc2 == EC - 1))
                            pe.inc(inst, 1)

                        sq_mm[0] = pe.n
                        ln2_state[tg] = pe.n

                    def emit_ln2b(tg):
                        tgc = 512 * tg
                        ST2_DONE = ln2_state[tg]

                        @blk.vector
                        def _(v, ST2_DONE=ST2_DONE):
                            v.wait_ge(pe.h, ST2_DONE)
                            v.tensor_scalar(out=m_row, in0=st2_s[:], scalar1=C_LNSC,
                                            scalar2=None, op0=ALU.mult)
                            dve.inc(v.tensor_scalar(out=s2_row, in0=st2_s2[:],
                                                    scalar1=C_LNSC, scalar2=None,
                                                    op0=ALU.mult), 1)
                            v.wait_ge(dve.h, dve.n)
                            dve.inc(v.tensor_tensor(out=msq_row, in0=m_row,
                                                    in1=m_row, op=ALU.mult), 1)
                            v.wait_ge(dve.h, dve.n)
                            dve.inc(v.tensor_tensor(out=var_row, in0=s2_row,
                                                    in1=msq_row,
                                                    op=ALU.subtract), 1)

                        V2 = dve.n

                        @blk.scalar
                        def _(sca, V2=V2):
                            sca.wait_ge(dve.h, V2)
                            act.inc(sca.activation(out=lnt_row, in_=var_row,
                                                   func=AF.Ln, bias=LN_EPS), 1)
                            sca.wait_ge(act.h, act.n)
                            act.inc(sca.activation(out=rstd_row, in_=lnt_row,
                                                   func=AF.Exp, scale=-0.5), 1)

                        RS2 = act.n

                        @blk.vector
                        def _(v, RS2=RS2):
                            v.wait_ge(act.h, RS2)
                            dve.inc(v.tensor_tensor(out=mrs_row, in0=m_row,
                                                    in1=rstd_row, op=ALU.mult), 1)
                            v.wait_ge(dve.h, dve.n)
                            v.tensor_copy(rstd_b, rstd_row)
                            dve.inc(v.tensor_copy(mrs_b, mrs_row), 1)

                        SM2 = dve.n

                        @blk.tensor
                        def _(t, SM2=SM2):
                            t.wait_ge(dve.h, SM2)
                            t.matmul(bc1c[:], onerow, rstd_b, start=True, stop=True)
                            pe.inc(t.matmul(bc2c[:], onerow, mrs_b, start=True,
                                            stop=True), 1)

                        BC2 = pe.n

                        @blk.vector
                        def _(v, tg=tg, tgc=tgc, BC2=BC2, ST2_DONE=ST2_DONE):
                            v.wait_ge(pe.h, BC2)
                            v.tensor_copy(rstd_bc2(tg), bc1c[:])
                            dve.inc(v.tensor_copy(mrs_bc2(tg), bc2c[:]), 1)
                            yb3 = yb_big[:].rearrange("p (e c) -> p e c", c=2048)[
                                :, :, tgc:tgc + 512]
                            gt3 = sqc[:].rearrange("p (e t) -> p e t", t=512)
                            rb3 = rstd_bc2(tg).rearrange(
                                "p t -> p () t").to_broadcast((128, EC, 512))
                            mb3 = mrs_bc2(tg).rearrange(
                                "p t -> p () t").to_broadcast((128, EC, 512))
                            v.wait_ge(dve.h, dve.n)
                            dve.inc(v.tensor_tensor(out=gt3, in0=yb3, in1=rb3,
                                                    op=ALU.mult), 1)
                            v.wait_ge(dve.h, dve.n)
                            dve.inc(v.tensor_tensor(out=yb3, in0=gt3, in1=mb3,
                                                    op=ALU.subtract), 1)

                    w_freeC = [0, 0]
                    p_freeC = [0, 0]
                    xf_free = [0, 0]
                    t1_free = [0, 0]
                    y_free = [0, 0]
                    xf_ready = [0, 0]

                    fi = 0
                    for half in range(2):
                        for fc in range(EC):
                            wbp = fc % 2

                            @blk.gpsimd
                            def _(g, fc=fc, wbp=wbp, WF=w_freeC[wbp]):
                                if WF:
                                    g.wait_ge(pe.h, WF)
                                src = woutT.rearrange("(ec e) f -> e ec f", e=128)[
                                    :, :, 128 * fc:128 * fc + 128]
                                dst = wb2[wbp][:].rearrange(
                                    "p (ec f) -> p ec f", f=128)
                                ld_w[wbp].inc(g.dma_start(out=dst, in_=src), 16)

                            WO_LD = ld_w[wbp].n

                            ii = fi % 2
                            fi += 1

                            @blk.sync
                            def _(sy, fc=fc, half=half, ii=ii, XF=xf_free[ii]):
                                if XF:
                                    sy.wait_ge(dve.h, XF)
                                ld_x[ii].inc(sy.dma_start(
                                    out=xf[ii][:],
                                    in_=xT[half, 128 * fc:128 * fc + 128, :]), 16)
                            xf_ready[ii] = ld_x[ii].n

                            @blk.tensor
                            def _(t, fc=fc, half=half, ii=ii, wbp=wbp,
                                  WO_LD=WO_LD, PF=p_freeC[ii]):
                                t.wait_ge(ld_w[wbp].h, WO_LD)
                                t.wait_ge(ld_g.h, ATTN_LD)
                                if PF:
                                    t.wait_ge(act.h, PF)
                                eord = ([e for e in range(EC) if e % 4 < 2] +
                                        [e for e in range(EC) if e % 4 >= 2])
                                for h2 in range(2):
                                    colc = 1024 * half + 512 * h2
                                    for k2, e in enumerate(eord):
                                        if (fc == 0 and half == 0
                                                and h2 == 0 and k2 == 8):
                                            t.wait_ge(ld_g2.h, ATTN_LD1)
                                        inst = t.matmul(
                                            wop[ii][:, 512 * h2:512 * h2 + 512],
                                            wb2[wbp][:, 128 * e:128 * e + 128],
                                            at_all[e][:, colc:colc + 512],
                                            start=(k2 == 0),
                                            stop=(k2 == EC - 1))
                                pe.inc(inst, 1)

                            WO_FILL = pe.n
                            w_freeC[wbp] = WO_FILL

                            @blk.scalar
                            def _(s, fc=fc, ii=ii, WO_FILL=WO_FILL,
                                  TF=t1_free[ii]):
                                s.wait_ge(pe.h, WO_FILL)
                                if TF:
                                    s.wait_ge(dve.h, TF)
                                act.inc(s.activation(
                                    out=t1[ii][:], in_=wop[ii][:],
                                    func=AF.Identity,
                                    bias=bo_sb[:, fc:fc + 1]), 1)

                            T1D = act.n
                            p_freeC[ii] = T1D

                            @blk.vector
                            def _(v, fc=fc, half=half, ii=ii, T1D=T1D,
                                  XR=xf_ready[ii], YF=y_free[ii]):
                                v.wait_ge(act.h, T1D)
                                v.wait_ge(ld_x[ii].h, XR)
                                if YF:
                                    v.wait_ge(st_y[ii].h, YF)
                                dve.inc(v.tensor_tensor(out=ysb[ii][:],
                                                        in0=t1[ii][:],
                                                        in1=xf[ii][:],
                                                        op=ALU.add), 1)
                                v.wait_ge(dve.h, dve.n)
                                dve.inc(v.tensor_copy(
                                    yb_all[fc][:, 1024 * half:1024 * half + 1024],
                                    ysb[ii][:]), 1)

                            xf_free[ii] = dve.n - 1
                            t1_free[ii] = dve.n - 1
                            Y_DONE = dve.n

                            @blk.sync
                            def _(sy, fc=fc, half=half, ii=ii, Y_DONE=Y_DONE):
                                sy.wait_ge(dve.h, Y_DONE)
                                st_y[ii].inc(sy.dma_start(
                                    out=yscr[128 * fc:128 * fc + 128,
                                             1024 * half:1024 * half + 1024],
                                    in_=ysb[ii][:]), 16)
                                y_free[ii] = st_y[ii].n

                            if half == 1:
                                if fc == 2:
                                    emit_ln2a(0)
                                elif fc == 4:
                                    emit_ln2b(0)
                                elif fc == 8:
                                    emit_ln2a(1)
                                elif fc == 10:
                                    emit_ln2b(1)

                    for tg in (2, 3):
                        emit_ln2a(tg)
                        emit_ln2b(tg)

                G_DONE = dve.n
                g_all = yb_all
                pat.close()
                full_barrier()

                with contextlib.ExitStack() as pg:
                    lgs = pg.enter_context(nc.sbuf_tensor("lgs", [128, 128], F32))
                    gch = pg.enter_context(nc.sbuf_tensor("gch", [128, 3 * 128 + 16], F32))
                    delta = pg.enter_context(nc.sbuf_tensor("delta", [128, 16], F32))
                    grow = pg.enter_context(nc.sbuf_tensor("grow", [128, 64], F32))
                    sumrwb = pg.enter_context(nc.sbuf_tensor("sumrwb", [128, 16], BF16))
                    srowb = pg.enter_context(nc.sbuf_tensor("srowb", [1, 2048], BF16))
                    scbc = pg.enter_context(nc.sbuf_tensor("scbc", [128, 2048], BF16))
                    pgp = pg.enter_context(contextlib.ExitStack())
                    lg_ps = pgp.enter_context(nc.psum_tensor("lg_ps", [128, 128], F32))
                    srow_ps = [pgp.enter_context(nc.psum_tensor(f"srp{i}", [1, 512], F32))
                               for i in range(4)]
                    scb_ps = pgp.enter_context(nc.psum_tensor("scb_ps", [128, 512], F32))

                    dvec = gch[:, 0:128]
                    maskc = gch[:, 128:256]
                    uc = gch[:, 256:384]
                    m1c = gch[:, 384:400]
                    e_t = grow[:, 0:16]
                    den_t = grow[:, 16:32]
                    rec_t = grow[:, 32:48]
                    t3_t = grow[:, 48:64]

                    @blk.tensor
                    def _(t):
                        t.wait_ge(dve.h, G_DONE)
                        t.wait_ge(ld_c.h, CONST_LD)
                        for w in range(16):
                            for e in range(EC):
                                inst = t.matmul(
                                    lg_ps[:, 8 * w:8 * w + 8],
                                    g_all[e][:, 128 * w:128 * w + 128],
                                    wg_sb[:, 8 * e:8 * e + 8],
                                    start=(e == 0), stop=(e == EC - 1))
                        pe.inc(inst, 1)

                    LG = pe.n

                    @blk.vector
                    def _(v, LG=LG):
                        v.wait_ge(pe.h, LG)
                        dve.inc(v.tensor_copy(lgs[:], lg_ps[:, :]), 1)
                        lg3 = lgs[:].rearrange("p (g x) -> p g x", x=8)
                        lg3t = lgs[:].rearrange("p (g x) -> p x g", x=8)
                        d3t = dvec[:].rearrange("p (g x) -> p x g", x=8)
                        mk3t = maskc[:].rearrange("p (g x) -> p x g", x=8)
                        u3t = uc[:].rearrange("p (g x) -> p x g", x=8)
                        u3 = uc[:].rearrange("p (g x) -> p g x", x=8)
                        m1b = m1c[:].rearrange("p g -> p () g").to_broadcast((128, 8, 16))
                        v.wait_ge(dve.h, dve.n)
                        dve.inc(v.tensor_reduce(out=m1c[:], in_=lg3, axis=AX.X,
                                                op=ALU.max), 1)
                        v.wait_ge(dve.h, dve.n)
                        dve.inc(v.tensor_tensor(out=d3t, in0=lg3t, in1=m1b,
                                                op=ALU.subtract), 1)
                        v.wait_ge(dve.h, dve.n)
                        dve.inc(v.tensor_scalar(out=mk3t, in0=d3t, scalar1=0.0,
                                                scalar2=None, op0=ALU.is_lt), 1)
                        v.wait_ge(dve.h, dve.n)
                        dve.inc(v.tensor_scalar(out=u3t, in0=d3t, scalar1=BIG,
                                                scalar2=None, op0=ALU.add), 1)
                        v.wait_ge(dve.h, dve.n)
                        dve.inc(v.tensor_tensor(out=u3t, in0=u3t, in1=mk3t,
                                                op=ALU.mult), 1)
                        v.wait_ge(dve.h, dve.n)
                        dve.inc(v.tensor_scalar(out=u3t, in0=u3t, scalar1=-BIG,
                                                scalar2=None, op0=ALU.add), 1)
                        v.wait_ge(dve.h, dve.n)
                        dve.inc(v.tensor_reduce(out=delta[:], in_=u3,
                                                axis=AX.X, op=ALU.max), 1)

                    DELTA = dve.n

                    @blk.scalar
                    def _(sca):
                        sca.wait_ge(dve.h, DELTA)
                        act.inc(sca.activation(out=e_t, in_=delta[:], func=AF.Exp), 1)

                    ET = act.n

                    @blk.vector
                    def _(v):
                        v.wait_ge(act.h, ET)
                        dve.inc(v.tensor_scalar(out=den_t, in0=e_t, scalar1=1.0,
                                                scalar2=None, op0=ALU.add), 1)
                        v.wait_ge(dve.h, dve.n)
                        dve.inc(v.reciprocal(out=rec_t, in_=den_t), 1)
                        v.wait_ge(dve.h, dve.n)
                        dve.inc(v.tensor_tensor(out=t3_t, in0=e_t, in1=rec_t,
                                                op=ALU.mult), 1)
                        v.wait_ge(dve.h, dve.n)
                        dve.inc(v.tensor_tensor(out=t3_t, in0=t3_t, in1=rec_t,
                                                op=ALU.add), 1)
                        v.wait_ge(dve.h, dve.n)
                        dve.inc(v.tensor_copy(sumrwb[:], t3_t), 1)

                    SRW = dve.n

                    @blk.tensor
                    def _(t):
                        t.wait_ge(dve.h, SRW)
                        for w in range(16):
                            inst = t.matmul(
                                srow_ps[w // 4][0:1, 128 * (w % 4):128 * (w % 4) + 128],
                                sumrwb[:, w:w + 1], id_sb[:], start=True, stop=True)
                        pe.inc(inst, 1)

                    SROW = pe.n

                    @blk.vector
                    def _(v):
                        v.wait_ge(pe.h, SROW)
                        for tg in range(4):
                            dve.inc(v.tensor_copy(srowb[0:1, 512 * tg:512 * tg + 512],
                                                  srow_ps[tg][0:1, :]), 1)

                    SRB = dve.n

                    for tg in range(4):
                        @blk.tensor
                        def _(t, tg=tg, SRB=SRB):
                            t.wait_ge(dve.h, SRB if tg == 0 else dve.n)
                            pe.inc(t.matmul(scb_ps[:], onerow,
                                            srowb[0:1, 512 * tg:512 * tg + 512],
                                            start=True, stop=True), 1)

                        SCB = pe.n

                        @blk.vector
                        def _(v, tg=tg, SCB=SCB):
                            v.wait_ge(pe.h, SCB)
                            dve.inc(v.tensor_copy(scbc[:, 512 * tg:512 * tg + 512],
                                                  scb_ps[:]), 1)

                    SCBC_DONE = dve.n
                    pgp.close()
                    full_barrier()

                    with contextlib.ExitStack() as ph:
                        hid_big = ph.enter_context(
                            nc.sbuf_tensor("hid_big", [128, EC * 2048], BF16))
                        hid_all = [hid_big[:, 2048 * i:2048 * i + 2048]
                                   for i in range(EC)]
                        hp = [ph.enter_context(nc.psum_tensor(f"hp{i}", [128, 1024], F32))
                              for i in range(2)]
                        hw_free = [0, 0]
                        hp_free = [0, 0]
                        fi2 = 0
                        for oc in range(EC):
                            wbp = oc % 2

                            @blk.gpsimd
                            def _(g, oc=oc, wbp=wbp, HWF=hw_free[wbp]):
                                if HWF:
                                    g.wait_ge(pe.h, HWF)
                                src = wupT.rearrange("(ec e) f -> e ec f", e=128)[
                                    :, :, 128 * oc:128 * oc + 128]
                                dst = wb2[wbp][:].rearrange("p (ec f) -> p ec f", f=128)
                                ld_w[wbp].inc(g.dma_start(out=dst, in_=src), 16)

                            HW_LD = ld_w[wbp].n

                            for half in range(2):
                                ii = fi2 % 2
                                fi2 += 1

                                @blk.tensor
                                def _(t, oc=oc, half=half, ii=ii, wbp=wbp,
                                      HW_LD=HW_LD, HPF=hp_free[ii]):
                                    t.wait_ge(ld_w[wbp].h, HW_LD)
                                    if HPF:
                                        t.wait_ge(dve.h, HPF)
                                    for h2 in range(2):
                                        colc = 1024 * half + 512 * h2
                                        for e in range(EC):
                                            inst = t.matmul(
                                                hp[ii][:, 512 * h2:512 * h2 + 512],
                                                wb2[wbp][:, 128 * e:128 * e + 128],
                                                g_all[e][:, colc:colc + 512],
                                                start=(e == 0), stop=(e == EC - 1))
                                    pe.inc(inst, 1)

                                HP_FILL = pe.n
                                if half == 1:
                                    hw_free[wbp] = HP_FILL

                                @blk.vector
                                def _(v, oc=oc, half=half, ii=ii, HP_FILL=HP_FILL):
                                    v.wait_ge(pe.h, HP_FILL)
                                    dve.inc(v.tensor_scalar(
                                        out=hid_all[oc][:, 1024 * half:1024 * half + 1024],
                                        in0=hp[ii][:], scalar1=0.0, scalar2=None,
                                        op0=ALU.max), 1)

                                hp_free[ii] = dve.n

                        HID_DONE = dve.n

                        with nc.sbuf_tensor("gs_t", [128, 8192], BF16) as gs_t:
                            for tg in range(NTG):
                                tgc = 512 * tg

                                @blk.vector
                                def _(v, tg=tg, tgc=tgc, first2=(tg == 0)):
                                    if first2:
                                        v.wait_ge(dve.h, HID_DONE)
                                        v.wait_ge(dve.h, SCBC_DONE)
                                    else:
                                        v.wait_ge(dve.h, dve.n)
                                    g3 = yb_big[:].rearrange("p (e c) -> p e c", c=2048)[
                                        :, :, tgc:tgc + 512]
                                    gs3 = gs_t[:].rearrange("p (e t) -> p e t", t=512)
                                    sc3 = scbc[:, tgc:tgc + 512].rearrange(
                                        "p t -> p () t").to_broadcast((128, EC, 512))
                                    dve.inc(v.tensor_tensor(out=gs3, in0=g3, in1=sc3,
                                                            op=ALU.mult), 1)
                                    v.wait_ge(dve.h, dve.n)
                                    dve.inc(v.tensor_copy(g3, gs3), 1)

                        GS_DONE = dve.n
                        full_barrier()

                        with contextlib.ExitStack() as p5:
                            yr = [p5.enter_context(
                                nc.sbuf_tensor(f"yr{i}", [128, 1024], F32))
                                for i in range(2)]
                            ot = [p5.enter_context(
                                nc.sbuf_tensor(f"ot{i}", [128, 1024], F32))
                                for i in range(2)]
                            fo = [p5.enter_context(
                                nc.psum_tensor(f"fo{i}", [128, 1024], F32))
                                for i in range(2)]
                            fw_free = [0, 0]
                            fo_free = [0, 0]
                            yr_free = [0, 0]
                            yr_ready = [0, 0]
                            ot_free = [0, 0]
                            YD0, YD1 = st_y[0].n, st_y[1].n
                            fi3 = 0
                            for fc in range(EC):
                                wbp = fc % 2

                                @blk.gpsimd
                                def _(g, fc=fc, wbp=wbp, FWF=fw_free[wbp]):
                                    if FWF:
                                        g.wait_ge(pe.h, FWF)
                                    src = wdnT.rearrange("(ec e) f -> e ec f", e=128)[
                                        :, :, 128 * fc:128 * fc + 128]
                                    dst = wb2[wbp][:].rearrange("p (ec f) -> p ec f", f=128)
                                    ld_w[wbp].inc(g.dma_start(out=dst, in_=src), 16)
                                    src2 = wexp.rearrange("(ec e) f -> e ec f", e=128)[
                                        :, :, 128 * fc:128 * fc + 128]
                                    dst2 = wb2e[wbp][:].rearrange("p (ec f) -> p ec f", f=128)
                                    ld_w[wbp].inc(g.dma_start(out=dst2, in_=src2), 16)

                                FW_LD = ld_w[wbp].n

                                for half in range(2):
                                    ii = fi3 % 2
                                    fi3 += 1

                                    @blk.sync
                                    def _(sy, fc=fc, half=half, ii=ii,
                                          YRF=yr_free[ii]):
                                        if YRF:
                                            sy.wait_ge(dve.h, YRF)
                                        if fc == 0 and half == 0:
                                            sy.wait_ge(st_y[0].h, YD0)
                                            sy.wait_ge(st_y[1].h, YD1)
                                        ld_x[ii].inc(sy.dma_start(
                                            out=yr[ii][:],
                                            in_=yscr[128 * fc:128 * fc + 128,
                                                     1024 * half:1024 * half + 1024]), 16)
                                    yr_ready[ii] = ld_x[ii].n

                                    @blk.tensor
                                    def _(t, fc=fc, half=half, ii=ii, wbp=wbp,
                                          FW_LD=FW_LD, FOF=fo_free[ii]):
                                        t.wait_ge(ld_w[wbp].h, FW_LD)
                                        if fc == 0 and half == 0:
                                            t.wait_ge(dve.h, GS_DONE)
                                        if FOF:
                                            t.wait_ge(dve.h, FOF)
                                        for h2 in range(2):
                                            colc = 1024 * half + 512 * h2
                                            for e in range(EC):
                                                t.matmul(
                                                    fo[ii][:, 512 * h2:512 * h2 + 512],
                                                    wb2[wbp][:, 128 * e:128 * e + 128],
                                                    hid_all[e][:, colc:colc + 512],
                                                    start=(e == 0), stop=False)
                                            for e in range(EC):
                                                inst = t.matmul(
                                                    fo[ii][:, 512 * h2:512 * h2 + 512],
                                                    wb2e[wbp][:, 128 * e:128 * e + 128],
                                                    g_all[e][:, colc:colc + 512],
                                                    start=False, stop=(e == EC - 1))
                                        pe.inc(inst, 1)

                                    FO_FILL = pe.n
                                    if half == 1:
                                        fw_free[wbp] = FO_FILL

                                    @blk.vector
                                    def _(v, ii=ii, FO_FILL=FO_FILL,
                                          YRR=yr_ready[ii], OTF=ot_free[ii]):
                                        v.wait_ge(pe.h, FO_FILL)
                                        v.wait_ge(ld_x[ii].h, YRR)
                                        if OTF:
                                            v.wait_ge(st_o[ii].h, OTF)
                                        dve.inc(v.tensor_tensor(
                                            out=ot[ii][:], in0=fo[ii][:],
                                            in1=yr[ii][:], op=ALU.add), 1)

                                    yr_free[ii] = dve.n
                                    fo_free[ii] = dve.n
                                    OT_D = dve.n

                                    @blk.scalar
                                    def _(sca, fc=fc, half=half, ii=ii, OT_D=OT_D):
                                        sca.wait_ge(dve.h, OT_D)
                                        st_o[ii].inc(sca.dma_start(
                                            out=outT[half, 128 * fc:128 * fc + 128, :],
                                            in_=ot[ii][:]), 16)
                                        ot_free[ii] = st_o[ii].n

        FIN0, FIN1 = st_o[0].n, st_o[1].n

        @blk.sync
        def _(sy):
            sy.wait_ge(st_o[0].h, FIN0)
            sy.wait_ge(st_o[1].h, FIN1)

        @blk.gpsimd
        def _(g):
            g.wait_ge(st_o[0].h, FIN0)
            g.wait_ge(st_o[1].h, FIN1)

    return nc


def _host_constants():
    dim = E
    inv_freq = np.float32(ROPE_BASE) ** (
        -(np.arange(0, dim, 2, dtype=np.float32) / np.float32(dim)))
    emb = np.repeat(inv_freq.astype(np.float32), 2)
    arg = (np.float32(S) * emb).astype(np.float32)
    sin = np.sin(arg).astype(np.float32)
    cos = np.cos(arg).astype(np.float32)

    mats = np.zeros((48, 128, 128), np.float32)
    for c in range(16):
        dcos = mats[3 * c]
        m1 = mats[3 * c + 1]
        m2 = mats[3 * c + 2]
        for i in range(128):
            p = 128 * c + i
            dcos[i, i] = cos[p]
            if c < 8:
                if i < 64:
                    m1[2 * i + 1, i] = -sin[p]
                else:
                    m2[2 * (i - 64) + 1, i] = -sin[p]
            else:
                if i < 64:
                    m1[2 * i, i] = sin[p]
                else:
                    m2[2 * (i - 64), i] = sin[p]
    mats_bf = mats.astype(ml_dtypes.bfloat16)

    masks = np.zeros((2, 128, 1024), np.float32)
    n = np.arange(512)
    p = np.arange(128)[:, None]
    for mi in range(2):
        for half in range(2):
            o = 256 * mi + 128 * half
            masks[mi, :, 512 * half:512 * half + 512] = (n[None, :] >= p + o)
    masks_bf = masks.astype(ml_dtypes.bfloat16)

    ident = np.eye(128, dtype=ml_dtypes.bfloat16)
    ones = np.ones((128, 128), dtype=ml_dtypes.bfloat16)
    cst = np.zeros((128, 8), np.float32)
    cst[:, 0] = np.float32(C_LNSC)
    cst[:, 1] = np.float32(LN_EPS)
    cst[:, 2] = np.float32(-0.5)
    cst[:, 3] = np.float32(0.5)
    return mats_bf, masks_bf, ident, ones, cst


def kernel(L0_attn_x, L0_attn_Wqkv, L0_attn_bqkv, L0_attn_Wout, L0_attn_bout,
           L0_ffn_W_up, L0_ffn_W_down, L0_ffn_W_gate, L0_ffn_W_expert):
    x = np.asarray(L0_attn_x, np.float32)
    mats_bf, masks_bf, ident, ones, cst = _host_constants()

    wqkvT = np.ascontiguousarray(np.asarray(L0_attn_Wqkv, np.float32).T)
    woutT = np.ascontiguousarray(np.asarray(L0_attn_Wout, np.float32).T)
    wupT = np.ascontiguousarray(np.asarray(L0_ffn_W_up, np.float32).T)
    wdnT = np.ascontiguousarray(np.asarray(L0_ffn_W_down, np.float32).T)
    wexp_ = np.ascontiguousarray(np.asarray(L0_ffn_W_expert, np.float32))
    wgT = np.ascontiguousarray(np.asarray(L0_ffn_W_gate, np.float32).T)
    bqkv2 = np.ascontiguousarray(
        np.asarray(L0_attn_bqkv, np.float32).reshape(48, 128).T)
    bout2 = np.ascontiguousarray(
        np.asarray(L0_attn_bout, np.float32).reshape(16, 128).T)

    xT = np.ascontiguousarray(x.transpose(0, 2, 1))

    nc = build()
    shared = {
        "wqkvT": wqkvT, "woutT": woutT, "wupT": wupT, "wdnT": wdnT,
        "wexp": wexp_, "wgT": wgT, "bqkv2": bqkv2, "bout2": bout2,
        "ropem": mats_bf, "masksd": masks_bf, "identd": ident,
        "onesd": ones, "cstd": cst,
    }
    in_maps = []
    for c in range(NCORES):
        m = dict(shared)
        m["xT"] = np.ascontiguousarray(xT[:, :, PB * c:PB * c + PB])
        in_maps.append(m)

    res = run_bass_kernel_spmd(nc, in_maps, core_ids=list(range(NCORES)))

    out = np.empty((B, S, E), np.float32)
    for c in range(NCORES):
        o = res.results[c]["outT"]
        out[:, PB * c:PB * c + PB, :] = np.asarray(o).transpose(0, 2, 1)
    if _DBG:
        kernel._dbg = [np.asarray(res.results[c]["dbg"]) for c in range(NCORES)]
    return out


# version 29 (final)
# speedup vs baseline: 1.1300x; 1.1300x over previous
import numpy as np
import ml_dtypes
import concourse.bass as bass
import concourse.mybir as mybir
from concourse.bass_utils import run_bass_kernel_spmd
from concourse.aot_env import allow_ici_all_to_all

F32 = mybir.dt.float32
BF16 = mybir.dt.bfloat16
AF = mybir.ActivationFunctionType
ALU = mybir.AluOpType
AX = mybir.AxisListType

NCORES = 8
B = 2
S = 8192
E = 2048
EC = 16
PB = 1024
TPC = 2048
NTG = 4
SCALE = 2048.0 ** -0.25
ROPE_BASE = 10000.0
LN_EPS = 1e-5
NORM_EPS = 1e-12
BIG = 1.0e9
C_LNSC = 1.0 / 2048.0

_DBG = None


class Sem:
    def __init__(self, h):
        self.h = h
        self.n = 0

    def inc(self, inst, k=1):
        inst.then_inc(self.h, k)
        self.n += k

    def inc_default(self, inst):
        inst.then_inc(self.h)
        self.n += 1


def _q_row(j, sl):
    return 512 * j + 128 * sl


def _kv_row(j, sl, typ):
    return 1024 * j + 256 * sl + 128 * typ


def build():
    nc = bass.Bass("TRN2", target_bir_lowering=False)

    xT = nc.dram_tensor("xT", [B, E, PB], F32, kind="ExternalInput").ap()
    wqkvT = nc.dram_tensor("wqkvT", [E, 3 * E], F32, kind="ExternalInput").ap()
    woutT = nc.dram_tensor("woutT", [E, E], F32, kind="ExternalInput").ap()
    wupT = nc.dram_tensor("wupT", [E, E], F32, kind="ExternalInput").ap()
    wdnT = nc.dram_tensor("wdnT", [E, E], F32, kind="ExternalInput").ap()
    wexp = nc.dram_tensor("wexp", [E, E], F32, kind="ExternalInput").ap()
    wgT = nc.dram_tensor("wgT", [E, 8], F32, kind="ExternalInput").ap()
    bqkv2 = nc.dram_tensor("bqkv2", [128, 48], F32, kind="ExternalInput").ap()
    bout2 = nc.dram_tensor("bout2", [128, 16], F32, kind="ExternalInput").ap()
    ropem = nc.dram_tensor("ropem", [48, 128, 128], BF16, kind="ExternalInput").ap()
    masksd = nc.dram_tensor("masksd", [2, 128, 1024], BF16, kind="ExternalInput").ap()
    identd = nc.dram_tensor("identd", [128, 128], BF16, kind="ExternalInput").ap()
    onesd = nc.dram_tensor("onesd", [128, 128], BF16, kind="ExternalInput").ap()
    cstd = nc.dram_tensor("cstd", [128, 8], F32, kind="ExternalInput").ap()
    outT = nc.dram_tensor("outT", [B, E, PB], F32, kind="ExternalOutput").ap()

    qch = [nc.dram_tensor("qch0", [4096, 512], BF16).ap(),
           nc.dram_tensor("qch1", [4096, 512], BF16).ap()]
    qgh = [nc.dram_tensor("qgh0", [4096, 512], BF16).ap(),
           nc.dram_tensor("qgh1", [4096, 512], BF16).ap()]
    kvc = nc.dram_tensor("kvc", [8192, 1024], BF16).ap()
    kvg = nc.dram_tensor("kvg", [8192, 1024], BF16).ap()
    aoc0 = nc.dram_tensor("aoc0", [2048, 1024], BF16).ap()
    aog0 = nc.dram_tensor("aog0", [2048, 1024], BF16).ap()
    aoc1 = nc.dram_tensor("aoc1", [2048, 1024], BF16).ap()
    aog1 = nc.dram_tensor("aog1", [2048, 1024], BF16).ap()
    yscr = nc.dram_tensor("yscr", [E, TPC], F32).ap()

    kvg_f = kvg.rearrange("a b -> (a b)")
    kvc_f = kvc.rearrange("a b -> (a b)")

    sems = {}
    sem_names = ["ld_c", "ld_g", "ld_w0", "ld_w1", "ld_kv0", "ld_kv1",
                 "ld_q0", "ld_q1", "ld_x0", "ld_x1",
                 "st_m0", "st_m1", "st_v0", "st_v1", "st_y0", "st_y1",
                 "st_o0", "st_o1", "cc1", "cc2", "cc3", "cc4", "cc5",
                 "pe", "act", "dve"]

    import contextlib
    with contextlib.ExitStack() as stack:
        id_sb = stack.enter_context(nc.sbuf_tensor("id_sb", [128, 128], BF16))
        ones_sb = stack.enter_context(nc.sbuf_tensor("ones_sb", [128, 128], BF16))
        bq_sb = stack.enter_context(nc.sbuf_tensor("bq_sb", [128, 48], F32))
        bo_sb = stack.enter_context(nc.sbuf_tensor("bo_sb", [128, 16], F32))
        wg_sb = stack.enter_context(nc.sbuf_tensor("wg_sb", [128, 128], BF16))
        cst_sb = stack.enter_context(nc.sbuf_tensor("cst_sb", [128, 8], F32))
        for s_ in sem_names:
            sems[s_] = Sem(stack.enter_context(nc.semaphore(s_)))
        blk = stack.enter_context(nc.Block())

        ld_c, ld_g = sems["ld_c"], sems["ld_g"]
        ld_g2 = ld_c
        ld_w = [sems["ld_w0"], sems["ld_w1"]]
        ld_kv = [sems["ld_kv0"], sems["ld_kv1"]]
        ld_q = [sems["ld_q0"], sems["ld_q1"]]
        ld_x = [sems["ld_x0"], sems["ld_x1"]]
        st_m = [sems["st_m0"], sems["st_m1"]]
        st_v = [sems["st_v0"], sems["st_v1"]]
        st_y = [sems["st_y0"], sems["st_y1"]]
        st_o = [sems["st_o0"], sems["st_o1"]]
        cc1, cc2 = sems["cc1"], sems["cc2"]
        cc3, cc4, cc5 = sems["cc3"], sems["cc4"], sems["cc5"]
        pe, act, dve = sems["pe"], sems["act"], sems["dve"]

        onecol = ones_sb[:, 0:1]
        onerow = ones_sb[0:1, :]

        def full_barrier():
            targets = [(s_, s_.n) for name_, s_ in sems.items()
                       if s_.n and not name_.startswith("cc")]
            for section in (blk.tensor, blk.scalar, blk.vector, blk.gpsimd,
                            blk.sync):
                @section
                def _(eng):
                    for s_, n_ in targets:
                        eng.wait_ge(s_.h, n_)

        nc.const_aps.register(F32, C_LNSC, cst_sb[:, 0:1])
        nc.const_aps.register(F32, LN_EPS, cst_sb[:, 1:2])
        nc.const_aps.register(F32, -0.5, cst_sb[:, 2:3])
        nc.const_aps.register(F32, 0.5, cst_sb[:, 3:4])

        @blk.sync
        def _(sy):
            ld_c.inc(sy.dma_start(out=id_sb[:], in_=identd), 16)
            ld_c.inc(sy.dma_start(out=ones_sb[:], in_=onesd), 16)
            ld_c.inc(sy.dma_start(out=bq_sb[:], in_=bqkv2), 16)
            ld_c.inc(sy.dma_start(out=bo_sb[:], in_=bout2), 16)
            ld_c.inc(sy.dma_start(out=cst_sb[:], in_=cstd), 16)

        @blk.gpsimd
        def _(g):
            src = wgT.rearrange("(ec e) g -> e ec g", e=128)
            dst = wg_sb[:].rearrange("p (ec g) -> p ec g", g=8)
            ld_c.inc(g.dma_start(out=dst, in_=src), 16)

        CONST_LD = ld_c.n

        with contextlib.ExitStack() as pa:
            rm_sb = pa.enter_context(nc.sbuf_tensor("rm_sb", [128, 48 * 128], BF16))
            h_big = pa.enter_context(nc.sbuf_tensor("h_big", [128, EC * TPC], BF16))
            h_all = [h_big[:, TPC * i:TPC * i + TPC] for i in range(EC)]

            @blk.sync
            def _(sy):
                src = ropem.rearrange("m p c -> p m c")
                dst = rm_sb[:].rearrange("p (m c) -> p m c", c=128)
                ld_c.inc(sy.dma_start(out=dst, in_=src), 16)

            ROPEM_LD = ld_c.n

            def rm(i):
                return rm_sb[:, 128 * i:128 * i + 128]

            if True:
                pl = pa.enter_context(contextlib.ExitStack())
                xb_ab = [pl.enter_context(
                    nc.sbuf_tensor(f"xb_a{i}", [128, 512 * 16], BF16))
                    for i in range(2)]
                sq_a = pl.enter_context(nc.sbuf_tensor("sq_a", [128, 512 * 16], BF16))
                lnrow = pl.enter_context(nc.sbuf_tensor("lnrow", [1, 512 * 7], F32))
                lnrow_b = pl.enter_context(nc.sbuf_tensor("lnrow_b", [1, 1024], BF16))
                lnbc = pl.enter_context(nc.sbuf_tensor("lnbc", [128, 1024], BF16))
                htmp = pl.enter_context(nc.sbuf_tensor("htmp", [128, 512], BF16))
                stat_ps = pl.enter_context(nc.psum_tensor("stat_ps", [1, 512], F32))
                stat_s = stat_ps[0:1, :]
                stat_s2 = stat_ps[0:1, :]
                bc12 = pl.enter_context(nc.psum_tensor("bc12", [128, 1024], F32))
                bc1 = bc12[:, 0:512]
                bc2 = bc12[:, 512:1024]

                xb_a = xb_ab[0]

                def xb(e):
                    return xb_a[:, 512 * e:512 * e + 512]

                def sqt(e):
                    return sq_a[:, 512 * e:512 * e + 512]

                m_row = lnrow[:, 0:512]
                s2_row = lnrow[:, 512:1024]
                msq_row = lnrow[:, 1024:1536]
                var_row = lnrow[:, 1536:2048]
                lnt_row = lnrow[:, 2048:2560]
                rstd_row = lnrow[:, 2560:3072]
                mrs_row = lnrow[:, 3072:3584]
                rstd_b = lnrow_b[:, 0:512]
                mrs_b = lnrow_b[:, 512:1024]
                rstd_bc = lnbc[:, 0:512]
                mrs_bc = lnbc[:, 512:1024]

                STATS_PREV = 0
                H_TG = {}
                XB_FREE = [None, None]
                for tg in range(NTG):
                    b, hf = tg // 2, tg % 2
                    xb_a = xb_ab[tg % 2]

                    @blk.gpsimd
                    def _(g, b=b, hf=hf, XF=XB_FREE[tg % 2]):
                        if XF:
                            g.wait_ge(pe.h, XF[0])
                            g.wait_ge(dve.h, XF[1])
                        for e in range(EC):
                            ld_g.inc(g.dma_start(
                                out=xb(e),
                                in_=xT[b, 128 * e:128 * e + 128,
                                       512 * hf:512 * hf + 512]), 16)

                    LD_XB = ld_g.n

                    @blk.vector
                    def _(v, xb_a=xb_a):
                        v.wait_ge(ld_g.h, LD_XB)
                        if STATS_PREV:
                            v.wait_ge(pe.h, STATS_PREV)
                        dve.inc(v.tensor_tensor(out=sq_a[:], in0=xb_a[:],
                                                in1=xb_a[:], op=ALU.mult), 1)

                    SQ_DONE = dve.n

                    @blk.tensor
                    def _(t):
                        t.wait_ge(ld_c.h, CONST_LD)
                        t.wait_ge(ld_g.h, LD_XB)
                        if dve.n:
                            t.wait_ge(dve.h, dve.n)
                        for e in range(EC):
                            inst = t.matmul(stat_s[:], onecol, xb(e),
                                            start=(e == 0), stop=(e == EC - 1))
                        pe.inc(inst, 1)

                    SUM_DONE = pe.n

                    @blk.vector
                    def _(v):
                        v.wait_ge(ld_c.h, CONST_LD)
                        v.wait_ge(pe.h, SUM_DONE)
                        dve.inc(v.tensor_scalar(out=m_row, in0=stat_s[:],
                                                scalar1=C_LNSC, scalar2=None,
                                                op0=ALU.mult), 1)

                    MR_DONE = dve.n

                    @blk.tensor
                    def _(t):
                        t.wait_ge(dve.h, max(MR_DONE, SQ_DONE))
                        for e in range(EC):
                            inst = t.matmul(stat_s2[:], onecol, sqt(e),
                                            start=(e == 0), stop=(e == EC - 1))
                        pe.inc(inst, 1)

                    STATS_DONE = pe.n
                    STATS_PREV = STATS_DONE

                    @blk.vector
                    def _(v):
                        v.wait_ge(pe.h, STATS_DONE)
                        dve.inc(v.tensor_scalar(out=s2_row, in0=stat_s2[:],
                                                scalar1=C_LNSC, scalar2=None,
                                                op0=ALU.mult), 1)
                        v.wait_ge(dve.h, dve.n)
                        dve.inc(v.tensor_tensor(out=msq_row, in0=m_row, in1=m_row,
                                                op=ALU.mult), 1)
                        v.wait_ge(dve.h, dve.n)
                        dve.inc(v.tensor_tensor(out=var_row, in0=s2_row, in1=msq_row,
                                                op=ALU.subtract), 1)

                    VAR_DONE = dve.n

                    @blk.scalar
                    def _(s):
                        s.wait_ge(dve.h, VAR_DONE)
                        act.inc(s.activation(out=lnt_row, in_=var_row, func=AF.Ln,
                                             bias=LN_EPS), 1)
                        s.wait_ge(act.h, act.n)
                        act.inc(s.activation(out=rstd_row, in_=lnt_row, func=AF.Exp,
                                             scale=-0.5), 1)

                    RSTD_DONE = act.n

                    @blk.vector
                    def _(v):
                        v.wait_ge(act.h, RSTD_DONE)
                        dve.inc(v.tensor_tensor(out=mrs_row, in0=m_row, in1=rstd_row,
                                                op=ALU.mult), 1)
                        v.wait_ge(dve.h, dve.n)
                        v.tensor_copy(rstd_b, rstd_row)
                        dve.inc(v.tensor_copy(mrs_b, mrs_row), 1)

                    SMALL_DONE = dve.n

                    @blk.tensor
                    def _(t):
                        t.wait_ge(dve.h, SMALL_DONE)
                        t.matmul(bc1[:], onerow, rstd_b, start=True, stop=True)
                        pe.inc(t.matmul(bc2[:], onerow, mrs_b, start=True, stop=True), 1)

                    BC_DONE = pe.n

                    @blk.vector
                    def _(v, tg=tg, xb_a=xb_a):
                        v.wait_ge(pe.h, BC_DONE)
                        v.tensor_copy(rstd_bc, bc1[:])
                        dve.inc(v.tensor_copy(mrs_bc, bc2[:]), 1)
                        v.wait_ge(dve.h, dve.n)
                        xb3 = xb_a[:].rearrange("p (e t) -> p e t", t=512)
                        sq3 = sq_a[:].rearrange("p (e t) -> p e t", t=512)
                        h3 = h_big[:].rearrange("p (e c) -> p e c", c=TPC)[
                            :, :, 512 * tg:512 * tg + 512]
                        rbc3 = rstd_bc.rearrange("p t -> p () t").to_broadcast(
                            (128, EC, 512))
                        mbc3 = mrs_bc.rearrange("p t -> p () t").to_broadcast(
                            (128, EC, 512))
                        dve.inc(v.tensor_tensor(out=sq3, in0=xb3, in1=rbc3,
                                                op=ALU.mult), 1)
                        v.wait_ge(dve.h, dve.n)
                        dve.inc(v.tensor_tensor(out=h3, in0=sq3, in1=mbc3,
                                                op=ALU.subtract), 1)
                    H_TG[tg] = dve.n
                    XB_FREE[tg % 2] = (STATS_DONE, dve.n)

            H_DONE = dve.n

            def emit_qkv_section(sec, tsb_all=None, h_gate=None, barrier=True,
                                 shared=None, S=None, pre_wld=None):
                fcs = {"v": range(32, 48), "k": range(16, 32), "q": range(0, 16)}[sec]
                if barrier:
                    full_barrier()
                with contextlib.ExitStack() as ps:
                    if shared is None:
                        wqs = [ps.enter_context(
                            nc.sbuf_tensor(f"wq{i}_{sec}", [128, 2048], BF16))
                            for i in range(2)]
                        qkp = [ps.enter_context(
                            nc.psum_tensor(f"qkp{i}_{sec}", [128, 1024], F32))
                            for i in range(2)]
                        w_free = [0, 0]
                        p_free = [0, 0]
                    else:
                        wqs, qkp = shared
                        w_free = S["w_free"]
                        p_free = S["p_free"]
                    if sec == "v":
                        v_sb = ps.enter_context(nc.sbuf_tensor("v_sb", [128, 1024], BF16))
                        vt_sb = ps.enter_context(nc.sbuf_tensor("vt_sb", [128, 1024], BF16))
                        vt_ps = ps.enter_context(nc.psum_tensor("vt_ps", [128, 512], BF16))

                    v_sb_free = 0
                    vt_ps_free = 0
                    vt_sb_free = [0, 0]

                    fill_i = 0
                    for idx, fc in enumerate(fcs):
                        wb = idx % 2
                        if pre_wld is not None and idx < len(pre_wld):
                            W_LD = pre_wld[idx]
                        else:
                            @blk.gpsimd
                            def _(g, fc=fc, wb=wb, WF=w_free[wb]):
                                if WF:
                                    g.wait_ge(pe.h, WF)
                                src = wqkvT.rearrange("(ec e) f -> e ec f", e=128)[
                                    :, :, 128 * fc:128 * fc + 128]
                                dst = wqs[wb][:].rearrange("p (ec f) -> p ec f", f=128)
                                ld_w[wb].inc(g.dma_start(out=dst, in_=src), 16)

                            W_LD = ld_w[wb].n

                        for hh in range(2):
                            r = fill_i % 2
                            fill_i += 1

                            @blk.tensor
                            def _(t, wb=wb, hh=hh, r=r, W_LD=W_LD, PF=p_free[r]):
                                t.wait_ge(ld_w[wb].h, W_LD)
                                t.wait_ge(dve.h, h_gate[hh] if h_gate else H_DONE)
                                if PF:
                                    t.wait_ge(dve.h, PF)
                                for h2 in range(2):
                                    tgc = 1024 * hh + 512 * h2
                                    for e in range(EC):
                                        inst = t.matmul(
                                            qkp[r][:, 512 * h2:512 * h2 + 512],
                                            wqs[wb][:, 128 * e:128 * e + 128],
                                            h_all[e][:, tgc:tgc + 512],
                                            start=(e == 0), stop=(e == EC - 1))
                                pe.inc(inst, 1)

                            FILL_DONE = pe.n
                            if hh == 1:
                                w_free[wb] = FILL_DONE

                            if sec == "v":
                                hd = fc - 32
                                p_pair = 16 * hh + hd
                                jj, ss = p_pair // 4, p_pair % 4
                                vbase = _kv_row(jj, ss, 1) * 1024

                                @blk.vector
                                def _(v, fc=fc, r=r, FILL_DONE=FILL_DONE,
                                      VF=v_sb_free):
                                    v.wait_ge(pe.h, FILL_DONE)
                                    if VF:
                                        v.wait_ge(pe.h, VF)
                                    dve.inc(v.tensor_scalar(
                                        out=v_sb[:], in0=qkp[r][:],
                                        scalar1=bq_sb[:, fc:fc + 1], scalar2=None,
                                        op0=ALU.add), 1)

                                EVAC = dve.n
                                p_free[r] = EVAC

                                for grp in range(2):
                                    @blk.tensor
                                    def _(t, grp=grp, EVAC=EVAC, VPF=vt_ps_free):
                                        t.wait_ge(dve.h, EVAC)
                                        if VPF:
                                            t.wait_ge(dve.h, VPF)
                                        for tb in range(4):
                                            tbg = 4 * grp + tb
                                            inst = t.transpose(
                                                vt_ps[:, 128 * tb:128 * tb + 128],
                                                v_sb[:, 128 * tbg:128 * tbg + 128],
                                                id_sb[:])
                                        pe.inc(inst, 1)

                                    TP_D = pe.n

                                    @blk.vector
                                    def _(v, grp=grp, TP_D=TP_D,
                                          VSF=vt_sb_free[grp]):
                                        v.wait_ge(pe.h, TP_D)
                                        if VSF:
                                            v.wait_ge(st_v[grp].h, VSF)
                                        dve.inc(v.tensor_copy(
                                            vt_sb[:, 512 * grp:512 * grp + 512],
                                            vt_ps[:]), 1)

                                    VT_C = dve.n
                                    vt_ps_free = VT_C

                                    @blk.scalar
                                    def _(s, grp=grp, VT_C=VT_C, vbase=vbase):
                                        s.wait_ge(dve.h, VT_C)
                                        dsta = kvc_f[
                                            vbase + 65536 * grp:
                                            vbase + 65536 * grp + 65536
                                        ].rearrange("(tb t d) -> t tb d",
                                                    tb=4, t=128, d=128)
                                        srca = vt_sb[:, 512 * grp:512 * grp + 512
                                                     ].rearrange("p (tb d) -> p tb d",
                                                                 d=128)
                                        st_v[grp].inc(s.dma_start(out=dsta, in_=srca), 16)
                                        vt_sb_free[grp] = st_v[grp].n

                                v_sb_free = pe.n
                            else:
                                ci = fc - (16 if sec == "k" else 0)

                                @blk.vector
                                def _(v, fc=fc, ci=ci, hh=hh, r=r, FILL_DONE=FILL_DONE):
                                    v.wait_ge(pe.h, FILL_DONE)
                                    dve.inc(v.tensor_scalar(
                                        out=tsb_all[ci][:, 1024 * hh:1024 * hh + 1024],
                                        in0=qkp[r][:],
                                        scalar1=bq_sb[:, fc:fc + 1], scalar2=None,
                                        op0=ALU.add), 1)

                                p_free[r] = dve.n

            def emit_rope_norm(sec, tsb_all, R):
                typ = 0 if sec == "q" else 1
                tg_order = [0, 2, 1, 3] if sec == "q" else [0, 1, 2, 3]
                EVAC_ALL = dve.n
                qrp, aux = R["qrp"], R["aux"]
                nsq_ps = aux[0:1, 0:512]
                qr_a, sq2, nr, nrb = R["qr_a"], R["sq2"], R["nr"], R["nrb"]
                rbc_sb, qh = R["rbc_sb"], R["qh"]
                QR_WAR, NSQ_WAR, qh_st = R["QR_WAR"], R["NSQ_WAR"], R["qh_st"]
                ln_t = nr[:, 512:1024]
                n_row = nr[:, 1024:1536]
                r_row = nr[:, 1536:2048]

                def qr(c):
                    return qr_a[:, 512 * c:512 * c + 512]

                for tg_i, tg in enumerate(tg_order):
                    b, hf = tg // 2, tg % 2
                    tgc = 512 * tg
                    act_base = act.n
                    dve_base = dve.n
                    EV = {c: act_base + c + 1 for c in range(EC)}
                    SQ = {c: dve_base + c + 1 for c in range(EC)}
                    NRB_DONE = dve_base + 16 + 4
                    rope_done = {}
                    nsq_done = {}
                    PRE_NW = dict(NSQ_WAR)

                    @blk.tensor
                    def _(t, tg=tg, tgc=tgc, tg_i=tg_i, EV=EV, SQ=SQ,
                          NRB_DONE=NRB_DONE, QW=dict(QR_WAR)):
                        def nsq(c):
                            t.wait_ge(dve.h, SQ[c])
                            inst = t.matmul(nsq_ps, onecol,
                                            sq2[:, 512 * (c % 3):512 * (c % 3) + 512],
                                            start=(c == 0), stop=(c == EC - 1))
                            pe.inc(inst, 1)
                            nsq_done[c] = pe.n
                            NSQ_WAR[c % 3] = pe.n

                        for c in range(EC):
                            src1 = 2 * c if c < 8 else 2 * (c - 8)
                            src2 = src1 + 1
                            rp = c % 3
                            if tg_i == 0 and c == 0:
                                t.wait_ge(ld_c.h, ROPEM_LD)
                                t.wait_ge(dve.h, EVAC_ALL)
                            if QW.get(rp):
                                t.wait_ge(act.h, QW[rp])
                            t.matmul(qrp[rp][:], rm(3 * c),
                                     tsb_all[c][:, tgc:tgc + 512],
                                     start=True, stop=False)
                            t.matmul(qrp[rp][:], rm(3 * c + 1),
                                     tsb_all[src1][:, tgc:tgc + 512],
                                     start=False, stop=False)
                            inst = t.matmul(qrp[rp][:], rm(3 * c + 2),
                                            tsb_all[src2][:, tgc:tgc + 512],
                                            start=False, stop=True)
                            pe.inc(inst, 1)
                            rope_done[c] = pe.n
                            QW[rp] = EV[c]
                            QR_WAR[rp] = EV[c]
                            if c >= 2:
                                nsq(c - 2)
                        nsq(EC - 2)
                        nsq(EC - 1)
                        t.wait_ge(dve.h, NRB_DONE)
                        pe.inc(t.matmul(aux[:, :], onerow, nrb[:],
                                        start=True, stop=True), 1)

                    RBC_DONE = pe.n
                    NSQ_ALL = nsq_done[EC - 1]

                    @blk.scalar
                    def _(s, NSQ_ALL=NSQ_ALL):
                        for c in range(EC):
                            s.wait_ge(pe.h, rope_done[c])
                            act.inc(s.activation(out=qr(c), in_=qrp[c % 3][:],
                                                 func=AF.Identity), 1)
                        s.wait_ge(pe.h, NSQ_ALL)
                        act.inc(s.activation(out=ln_t, in_=nsq_ps, func=AF.Ln), 1)
                        s.wait_ge(act.h, act.n)
                        act.inc(s.activation(out=n_row, in_=ln_t, func=AF.Exp,
                                             scale=0.5), 1)

                    N_DONE = act.n
                    assert N_DONE == act_base + 18

                    @blk.vector
                    def _(v, N_DONE=N_DONE, EV=EV, PRE_NW=PRE_NW):
                        for c in range(EC):
                            v.wait_ge(act.h, EV[c])
                            if c >= 3:
                                v.wait_ge(pe.h, nsq_done[c - 3])
                            elif PRE_NW.get(c % 3):
                                v.wait_ge(pe.h, PRE_NW[c % 3])
                            dve.inc(v.tensor_tensor(
                                out=sq2[:, 512 * (c % 3):512 * (c % 3) + 512],
                                in0=qr(c), in1=qr(c), op=ALU.mult), 1)
                        v.wait_ge(act.h, N_DONE)
                        dve.inc(v.tensor_scalar(out=n_row, in0=n_row,
                                                scalar1=NORM_EPS, scalar2=None,
                                                op0=ALU.max), 1)
                        v.wait_ge(dve.h, dve.n)
                        dve.inc(v.reciprocal(out=r_row, in_=n_row), 1)
                        v.wait_ge(dve.h, dve.n)
                        dve.inc(v.tensor_scalar(out=r_row, in0=r_row,
                                                scalar1=float(SCALE), scalar2=None,
                                                op0=ALU.mult), 1)
                        v.wait_ge(dve.h, dve.n)
                        dve.inc(v.tensor_copy(nrb[:], r_row), 1)

                    assert dve.n == NRB_DONE, (dve.n, NRB_DONE)

                    @blk.vector
                    def _(v, RBC_DONE=RBC_DONE, QS0=qh_st[0], QS1=qh_st[1]):
                        v.wait_ge(pe.h, RBC_DONE)
                        if QS0:
                            v.wait_ge(st_m[0].h, QS0)
                        if QS1:
                            v.wait_ge(st_m[1].h, QS1)
                        dve.inc(v.tensor_copy(rbc_sb[:], aux[:, :]), 1)
                        for c in range(EC):
                            hp = c % 4
                            dve.inc(v.tensor_tensor(
                                out=qh[:, 512 * hp:512 * hp + 512],
                                in0=qr(c), in1=rbc_sb[:], op=ALU.mult), 1)

                    QH_BASE = dve.n - EC

                    @blk.sync
                    def _(sy, b=b, hf=hf, QH_BASE=QH_BASE):
                        for c in range(EC):
                            hp = c % 4
                            sy.wait_ge(dve.h, QH_BASE + c + 1)
                            p_pair = 16 * b + c
                            jj, ss = p_pair // 4, p_pair % 4
                            if typ == 0:
                                dst = qch[hf][_q_row(jj, ss):_q_row(jj, ss) + 128, :]
                            else:
                                row = _kv_row(jj, ss, 0)
                                dst = kvc[row:row + 128, 512 * hf:512 * hf + 512]
                            st_m[hp % 2].inc(sy.dma_start(
                                out=dst,
                                in_=qh[:, 512 * hp:512 * hp + 512]), 16)

                    qh_st[0], qh_st[1] = st_m[0].n, st_m[1].n
                    if sec == "q" and tg_i in (1, 3):
                        hf_done = 0 if tg_i == 1 else 1
                        ccq = cc3 if hf_done == 0 else cc5
                        QM0, QM1 = st_m[0].n, st_m[1].n

                        @blk.gpsimd
                        def _(gp, hf_done=hf_done, ccq=ccq, QM0=QM0, QM1=QM1):
                            gp.wait_ge(st_m[0].h, QM0)
                            gp.wait_ge(st_m[1].h, QM1)
                            with allow_ici_all_to_all():
                                ccq.inc_default(gp.collective_compute(
                                    "AllToAll", ALU.bypass,
                                    replica_groups=[list(range(NCORES))],
                                    ins=[qch[hf_done].opt()],
                                    outs=[qgh[hf_done].opt()]))

            emit_qkv_section("v", h_gate=(H_TG[1], H_TG[3]), barrier=False)
            pl.close()
            full_barrier()
            with contextlib.ExitStack() as pa2:
                tsb_big = pa2.enter_context(
                    nc.sbuf_tensor("tsb_big", [128, EC * 2048], BF16))
                tsb_sh = [tsb_big[:, 2048 * i:2048 * i + 2048] for i in range(EC)]
                wq_sh = [pa2.enter_context(
                    nc.sbuf_tensor(f"wqsh{i}", [128, 2048], BF16)) for i in range(2)]
                qkp_sh = [pa2.enter_context(
                    nc.psum_tensor(f"qkpsh{i}", [128, 1024], F32)) for i in range(2)]
                R = {
                    "qr_a": pa2.enter_context(
                        nc.sbuf_tensor("qr_sh", [128, 512 * 16], BF16)),
                    "sq2": pa2.enter_context(
                        nc.sbuf_tensor("sq2_sh", [128, 1536], BF16)),
                    "nr": pa2.enter_context(nc.sbuf_tensor("nr_sh", [1, 2048], F32)),
                    "nrb": pa2.enter_context(nc.sbuf_tensor("nrb_sh", [1, 512], BF16)),
                    "rbc_sb": pa2.enter_context(
                        nc.sbuf_tensor("rbc_sh", [128, 512], BF16)),
                    "qh": pa2.enter_context(nc.sbuf_tensor("qh_sh", [128, 2048], BF16)),
                    "qrp": [pa2.enter_context(
                        nc.psum_tensor(f"qrpsh{i}", [128, 512], F32)) for i in range(3)],
                    "aux": pa2.enter_context(nc.psum_tensor("aux_sh", [128, 512], F32)),
                    "QR_WAR": {}, "NSQ_WAR": {}, "qh_st": [0, 0],
                }
                S = {"w_free": [0, 0], "p_free": [0, 0]}
                emit_qkv_section("k", tsb_all=tsb_sh, barrier=False,
                                 shared=(wq_sh, qkp_sh), S=S)
                emit_rope_norm("k", tsb_sh, R)

                pre_wld = []
                for idx in range(2):
                    @blk.gpsimd
                    def _(g, fc=idx, wb=idx, WF=S["w_free"][idx]):
                        if WF:
                            g.wait_ge(pe.h, WF)
                        src = wqkvT.rearrange("(ec e) f -> e ec f", e=128)[
                            :, :, 128 * fc:128 * fc + 128]
                        dst = wq_sh[wb][:].rearrange("p (ec f) -> p ec f", f=128)
                        ld_w[wb].inc(g.dma_start(out=dst, in_=src), 16)
                    pre_wld.append(ld_w[idx].n)
                    S["w_free"][idx] = 0

                KV_M0, KV_M1 = st_m[0].n, st_m[1].n
                KV_V0, KV_V1 = st_v[0].n, st_v[1].n

                @blk.gpsimd
                def _(g):
                    g.wait_ge(st_m[0].h, KV_M0)
                    g.wait_ge(st_m[1].h, KV_M1)
                    g.wait_ge(st_v[0].h, KV_V0)
                    g.wait_ge(st_v[1].h, KV_V1)
                    with allow_ici_all_to_all():
                        cc1.inc_default(g.collective_compute(
                            "AllToAll", ALU.bypass,
                            replica_groups=[list(range(NCORES))],
                            ins=[kvc.opt()], outs=[kvg.opt()]))

                emit_qkv_section("q", tsb_all=tsb_sh, barrier=False,
                                 shared=(wq_sh, qkp_sh), S=S, pre_wld=pre_wld)
                emit_rope_norm("q", tsb_sh, R)



        full_barrier()
        with contextlib.ExitStack() as pb_:
            kc = [pb_.enter_context(nc.sbuf_tensor(f"kc{i}", [128, 8192], BF16))
                  for i in range(2)]
            vc = [pb_.enter_context(nc.sbuf_tensor(f"vc{i}", [128, 8192], BF16))
                  for i in range(2)]
            qb2 = pb_.enter_context(nc.sbuf_tensor("qb2", [128, 1024], BF16))
            pbuf = [pb_.enter_context(nc.sbuf_tensor(f"pb{i}", [128, 1024], BF16))
                    for i in range(4)]
            pmb = [pb_.enter_context(nc.sbuf_tensor(f"pm{i}", [128, 1024], BF16))
                   for i in range(2)]
            asum = [pb_.enter_context(nc.sbuf_tensor(f"asum{i}", [128, 1024], F32))
                    for i in range(2)]
            ptmp = pb_.enter_context(nc.sbuf_tensor("ptmp", [128, 1024], BF16))
            mask_sb = pb_.enter_context(nc.sbuf_tensor("mask_sb", [128, 2048], BF16))
            asum_bf = pb_.enter_context(nc.sbuf_tensor("asum_bf", [128, 1024], BF16))
            ep_row = pb_.enter_context(nc.sbuf_tensor("ep_row", [1, 1024], F32))
            ep_rowb = pb_.enter_context(nc.sbuf_tensor("ep_rowb", [1, 512], BF16))
            rbcB = pb_.enter_context(nc.sbuf_tensor("rbcB", [128, 512], BF16))
            ao = [pb_.enter_context(nc.sbuf_tensor(f"ao{i}", [128, 512], BF16))
                  for i in range(2)]
            scp = [pb_.enter_context(nc.psum_tensor(f"scp{i}", [128, 1024], F32))
                   for i in range(2)]
            accat = [pb_.enter_context(nc.psum_tensor(f"accat{i}", [128, 512], F32))
                     for i in range(2)]
            accA = [pb_.enter_context(nc.psum_tensor(f"accA{i}", [128, 512], F32))
                    for i in range(2)]

            @blk.sync
            def _(sy):
                src = masksd.rearrange("m p c -> p m c")
                dst = mask_sb[:].rearrange("p (m c) -> p m c", c=1024)
                ld_c.inc(sy.dma_start(out=dst, in_=src), 16)

            MASK_LD = ld_c.n

            sc_free = [0, 0]
            pb_war = [None, None, None, None]
            pm_free = [0, 0]
            qb_ready = [0, 0]
            qb_free = [0, 0]
            ao_free = [0, 0]
            epi_done = [0, 0]
            kv_ready = [0, 0]
            pair_pe_end = [0, 0, 0, 0]

            def emit_kv_loads(s):
                bufp = s % 2

                @blk.gpsimd
                def _(g, s=s, bufp=bufp):
                    if s == 0:
                        g.wait_ge(cc1.h, 1)
                    if s >= 2:
                        g.wait_ge(pe.h, pair_pe_end[s - 2])
                    for gch in range(16):
                        r, hf = gch // 2, gch % 2
                        krow = _kv_row(r, s, 0)
                        ld_kv[bufp].inc(g.dma_start(
                            out=kc[bufp][:, 512 * gch:512 * gch + 512],
                            in_=kvg[krow:krow + 128, 512 * hf:512 * hf + 512]), 16)
                        vbase = _kv_row(r, s, 1) * 1024 + 65536 * hf
                        src = kvg_f[vbase:vbase + 65536].rearrange(
                            "(u t d) -> t u d", u=4, t=128, d=128)
                        dst = vc[bufp][:, 512 * gch:512 * gch + 512].rearrange(
                            "p (u d) -> p u d", d=128)
                        ld_kv[bufp].inc(g.dma_start(out=dst, in_=src), 16)
                kv_ready[bufp] = ld_kv[bufp].n

            emit_kv_loads(0)
            emit_kv_loads(1)

            sched = [(s, g) for s in range(4) for g in range(16)]

            def emit_qb_load(i):
                s, g = sched[i]
                par = i % 2
                r, hf = g // 2, g % 2

                @blk.sync
                def _(sy, s=s, g=g, par=par, r=r, hf=hf, i=i, QF=qb_free[par]):
                    if i == 0:
                        sy.wait_ge(cc3.h, 1)
                    if i == 1:
                        sy.wait_ge(cc5.h, 1)
                    if QF:
                        sy.wait_ge(pe.h, QF)
                    qrow = _q_row(r, s)
                    ld_q[par].inc(sy.dma_start(
                        out=qb2[:, 512 * par:512 * par + 512],
                        in_=qgh[hf][qrow:qrow + 128, :]), 16)
                qb_ready[par] = ld_q[par].n

            emit_qb_load(0)

            pend = None
            epi_mm_prev = {}

            def emit_pend_store(pend_, EPI_F_val):
                @blk.sync
                def _(sy, pend_=pend_, EPI_F_val=EPI_F_val):
                    sy.wait_ge(dve.h, EPI_F_val)
                    dstT = aoc0 if pend_["s"] < 2 else aoc1
                    row = 256 * pend_["r"] + 128 * (pend_["s"] % 2)
                    st_m[pend_["gpar"]].inc(sy.dma_start(
                        out=dstT[row:row + 128,
                                 512 * pend_["hf"]:512 * pend_["hf"] + 512],
                        in_=ao[pend_["gpar"]][:]), 16)
                    ao_free[pend_["gpar"]] = st_m[pend_["gpar"]].n
                if pend_["s"] == 1 and pend_["g"] == 15:
                    A0_M0, A0_M1 = st_m[0].n, st_m[1].n

                    @blk.gpsimd
                    def _(gp, A0_M0=A0_M0, A0_M1=A0_M1):
                        gp.wait_ge(st_m[0].h, A0_M0)
                        gp.wait_ge(st_m[1].h, A0_M1)
                        with allow_ici_all_to_all():
                            cc2.inc_default(gp.collective_compute(
                                "AllToAll", ALU.bypass,
                                replica_groups=[list(range(NCORES))],
                                ins=[aoc0.opt()], outs=[aog0.opt()]))

            act_sim = act.n
            sc_defer = []
            dve_sim = dve.n
            dv_defer = []
            hoist_rec = None

            for i, (s, g) in enumerate(sched):
                bufp = s % 2
                gpar = i % 2
                r, hf = g // 2, g % 2
                if i + 1 < len(sched):
                    emit_qb_load(i + 1)

                npair = 2 * g + 2
                act_base = act_sim
                dve_base = dve_sim
                nrec = 2 if pend else 0
                exp_done = {tp: act_base + tp + 1 + (nrec if tp >= 2 else 0)
                            for tp in range(npair)}
                REC_prev = act_base + nrec + 2
                act_sim += nrec + npair
                off = 1 if pend else 0
                EPI_A_prev = dve_base + 1
                EPI_F_prev = dve_base + 7
                mask_done = {0: dve_base + off + 1, 1: dve_base + off + 2}
                presum_done = {0: dve_base + off + 3, 1: dve_base + off + 3}
                post = dve_base + off + 4 + (2 if pend else 0)
                for ke in range(2, npair, 2):
                    presum_done[ke] = post + (ke - 2) + 1
                    presum_done[ke + 1] = post + (ke - 2) + 1
                DVE_END = post + (npair - 2)
                dve_sim = DVE_END

                scores_done = {}
                iav_done = {}
                exp_war = {}
                epi_mm = {}
                ED_prev = epi_done[gpar]
                PM_prev = list(pm_free)
                QR = qb_ready[gpar]
                KR = kv_ready[bufp]

                hoisted = hoist_rec
                hoist_rec = None
                if hoisted is not None:
                    scores_done[0], exp_war[0] = hoisted[0]
                    scores_done[1], exp_war[1] = hoisted[1]
                have_next = i + 1 < len(sched)
                if have_next:
                    ns_, ng_ = sched[i + 1]
                    nbufp, ngpar = ns_ % 2, (i + 1) % 2
                    NQR = qb_ready[ngpar]
                    NKR = kv_ready[nbufp]
                    exp_done_next = {0: act_sim + 1, 1: act_sim + 2}
                    nmask_done = {0: dve_sim + 2, 1: dve_sim + 3}
                    hoist_rec = {}

                @blk.tensor
                def _(t, s=s, g=g, gpar=gpar, bufp=bufp, npair=npair,
                      ED_prev=ED_prev, QR=QR, KR=KR, pend=pend,
                      EPI_A_prev=EPI_A_prev, REC_prev=REC_prev,
                      hoisted=hoisted, hoist_rec=hoist_rec,
                      have_next=have_next):
                    def iav(tp):
                        p2 = tp % 2
                        p4 = tp % 4
                        diag = tp < 2
                        srcb = pmb[p2] if diag else pbuf[p4]
                        if diag:
                            t.wait_ge(dve.h, mask_done[tp])
                        else:
                            t.wait_ge(act.h, exp_done[tp])
                        if tp == 0 and ED_prev:
                            t.wait_ge(dve.h, ED_prev)
                        p_ = (2 * g + tp) if tp < 2 else (tp - 2)
                        j0, j1 = 2 * p_, 2 * p_ + 1
                        t.matmul(accat[gpar][:],
                                 vc[bufp][:, 128 * j0:128 * j0 + 128],
                                 srcb[:, 0:512], start=(tp == 0), stop=False)
                        inst = t.matmul(accat[gpar][:],
                                        vc[bufp][:, 128 * j1:128 * j1 + 128],
                                        srcb[:, 512:1024],
                                        start=False, stop=(tp == npair - 1))
                        pe.inc(inst, 1)
                        iav_done[tp] = pe.n
                        if diag:
                            pm_free[p2] = pe.n

                    def sco(tp):
                        p2 = tp % 2
                        if sc_free[p2]:
                            t.wait_ge(act.h, sc_free[p2])
                        if tp == 0:
                            t.wait_ge(ld_q[gpar].h, QR)
                            t.wait_ge(ld_kv[bufp].h, KR)
                        p_ = (2 * g + tp) if tp < 2 else (tp - 2)
                        j0, j1 = 2 * p_, 2 * p_ + 1
                        t.matmul(scp[p2][:, 0:512],
                                 kc[bufp][:, 128 * j0:128 * j0 + 128],
                                 qb2[:, 512 * gpar:512 * gpar + 512],
                                 start=True, stop=True)
                        inst = t.matmul(scp[p2][:, 512:1024],
                                        kc[bufp][:, 128 * j1:128 * j1 + 128],
                                        qb2[:, 512 * gpar:512 * gpar + 512],
                                        start=True, stop=True)
                        pe.inc(inst, 1)
                        scores_done[tp] = pe.n
                        exp_war[tp] = pb_war[tp % 4]
                        if tp < 2:
                            pb_war[tp % 4] = [(dve, mask_done[tp])]
                        sc_free[p2] = exp_done[tp]

                    def epi_s():
                        t.wait_ge(dve.h, EPI_A_prev)
                        t.matmul(accA[pend["gpar"]][0:1, 0:512],
                                 onecol, asum_bf[:, 0:512],
                                 start=True, stop=False)
                        pe.inc(t.matmul(accA[pend["gpar"]][0:1, 0:512],
                                        onecol, asum_bf[:, 512:1024],
                                        start=False, stop=True), 1)
                        epi_mm["S"] = pe.n

                    def sco_next(tp):
                        p2 = tp % 2
                        if sc_free[p2]:
                            t.wait_ge(act.h, sc_free[p2])
                        if tp == 0:
                            t.wait_ge(ld_q[ngpar].h, NQR)
                            t.wait_ge(ld_kv[nbufp].h, NKR)
                        p_ = 2 * ng_ + tp
                        j0, j1 = 2 * p_, 2 * p_ + 1
                        t.matmul(scp[p2][:, 0:512],
                                 kc[nbufp][:, 128 * j0:128 * j0 + 128],
                                 qb2[:, 512 * ngpar:512 * ngpar + 512],
                                 start=True, stop=True)
                        inst = t.matmul(scp[p2][:, 512:1024],
                                        kc[nbufp][:, 128 * j1:128 * j1 + 128],
                                        qb2[:, 512 * ngpar:512 * ngpar + 512],
                                        start=True, stop=True)
                        pe.inc(inst, 1)
                        hoist_rec[tp] = (pe.n, pb_war[tp % 2])
                        pb_war[tp % 2] = [(dve, nmask_done[tp])]
                        sc_free[p2] = exp_done_next[tp]

                    if hoisted is None:
                        sco(0)
                        sco(1)
                    if pend is not None and npair == 2:
                        epi_s()
                    for tp in range(npair):
                        if tp + 2 < npair:
                            sco(tp + 2)
                        if pend is not None and npair > 2 and tp == 0:
                            epi_s()
                        if pend is not None and tp == 1:
                            t.wait_ge(act.h, REC_prev)
                            pe.inc(t.matmul(accA[pend["gpar"]][:, :],
                                            onerow, ep_rowb[:],
                                            start=True, stop=True), 1)
                            epi_mm["B"] = pe.n
                        if tp == npair - 2 and have_next:
                            sco_next(0)
                            sco_next(1)
                        iav(tp)
                        if tp >= 2:
                            pb_war[tp % 4] = [(pe, iav_done[tp]),
                                              (dve, presum_done[tp])]

                LAST_IAV = iav_done[npair - 1]
                qb_free[gpar] = pe.n

                sc_defer.append(dict(npair=npair,
                                     pend_gpar=(pend["gpar"] if pend else None),
                                     epi_mm=epi_mm, scores_done=scores_done,
                                     exp_war=exp_war))

                dv_defer.append(dict(
                    g=g, gpar=gpar, PM_prev=PM_prev, exp_done=exp_done,
                    pend_gpar=(pend["gpar"] if pend else None),
                    pend_LAST_IAV=(pend["LAST_IAV"] if pend else 0),
                    AF_=(ao_free[pend["gpar"]] if pend else 0),
                    epi_mm=epi_mm, DVE_END=DVE_END))
                if pend is not None:
                    epi_done[pend["gpar"]] = EPI_F_prev
                    emit_pend_store(pend, EPI_F_prev)
                pend = {"gpar": gpar, "LAST_IAV": LAST_IAV,
                        "s": s, "g": g, "r": r, "hf": hf}

                if g == 15:
                    pair_pe_end[s] = pe.n
                    if s + 2 < 4:
                        emit_kv_loads(s + 2)
                    vrecs = dv_defer
                    dv_defer = []

                    @blk.vector
                    def _(v, vrecs=vrecs):
                        for rc in vrecs:
                            rg, rgpar = rc["g"], rc["gpar"]
                            ed = rc["exp_done"]
                            pg = rc["pend_gpar"]
                            if pg is not None:
                                v.wait_ge(pe.h, rc["pend_LAST_IAV"])
                                dve.inc(v.tensor_copy(asum_bf[:],
                                                      asum[pg][:]), 1)
                            for mi in range(2):
                                v.wait_ge(act.h, ed[mi])
                                v.wait_ge(ld_c.h, MASK_LD)
                                if rc["PM_prev"][mi]:
                                    v.wait_ge(pe.h, rc["PM_prev"][mi])
                                dve.inc(v.tensor_tensor(
                                    out=pmb[mi][:], in0=pbuf[mi][:],
                                    in1=mask_sb[:, 1024 * mi:1024 * mi + 1024],
                                    op=ALU.mult), 1)
                            dve.inc(v.tensor_tensor(out=ptmp[:], in0=pmb[0][:],
                                                    in1=pmb[1][:],
                                                    op=ALU.add), 1)
                            dve.inc(v.tensor_copy(asum[rgpar][:], ptmp[:]), 1)
                            if pg is not None:
                                v.wait_ge(pe.h, rc["epi_mm"]["B"])
                                dve.inc(v.tensor_copy(rbcB[:],
                                                      accA[pg][:, :]), 1)
                                v.wait_ge(dve.h, dve.n)
                                if rc["AF_"]:
                                    v.wait_ge(st_m[pg].h, rc["AF_"])
                                dve.inc(v.tensor_tensor(out=ao[pg][:],
                                                        in0=accat[pg][:],
                                                        in1=rbcB[:],
                                                        op=ALU.mult), 1)
                            for te in range(2, 2 * rg + 2, 2):
                                v.wait_ge(act.h, ed[te + 1])
                                dve.inc(v.tensor_tensor(
                                    out=ptmp[:], in0=pbuf[te % 4][:],
                                    in1=pbuf[(te + 1) % 4][:],
                                    op=ALU.add), 1)
                                dve.inc(v.tensor_tensor(
                                    out=asum[rgpar][:],
                                    in0=asum[rgpar][:],
                                    in1=ptmp[:], op=ALU.add), 1)
                            assert dve.n == rc["DVE_END"], (dve.n, rc)

                    recs = sc_defer
                    sc_defer = []

                    @blk.scalar
                    def _(sca, recs=recs):
                        for rc in recs:
                            def one_exp(tp):
                                p2 = tp % 2
                                sca.wait_ge(pe.h, rc["scores_done"][tp])
                                if rc["exp_war"][tp]:
                                    for wsem, wn in rc["exp_war"][tp]:
                                        if wn:
                                            sca.wait_ge(wsem.h, wn)
                                act.inc(sca.activation(out=pbuf[tp % 4][:],
                                                       in_=scp[p2][:],
                                                       func=AF.Exp), 1)

                            one_exp(0)
                            one_exp(1)
                            if rc["pend_gpar"] is not None:
                                sca.wait_ge(pe.h, rc["epi_mm"]["S"])
                                act.inc(sca.activation(
                                    out=ep_row[:, 0:512],
                                    in_=accA[rc["pend_gpar"]][0:1, 0:512],
                                    func=AF.Ln), 1)
                                sca.wait_ge(act.h, act.n)
                                act.inc(sca.activation(
                                    out=ep_rowb[:], in_=ep_row[:, 0:512],
                                    func=AF.Exp, scale=-1.0), 1)
                            for tp in range(2, rc["npair"]):
                                one_exp(tp)

                    assert act.n == act_sim, (act.n, act_sim)

            dve_base = dve.n
            EPI_A_f, EPI_F_f = dve_base + 1, dve_base + 3
            REC_f = act.n + 2
            epi_mm = {}

            @blk.tensor
            def _(t, pend=pend):
                t.wait_ge(dve.h, EPI_A_f)
                t.matmul(accA[pend["gpar"]][0:1, 0:512], onecol,
                         asum_bf[:, 0:512], start=True, stop=False)
                pe.inc(t.matmul(accA[pend["gpar"]][0:1, 0:512], onecol,
                                asum_bf[:, 512:1024], start=False, stop=True), 1)
                epi_mm["S"] = pe.n
                t.wait_ge(act.h, REC_f)
                pe.inc(t.matmul(accA[pend["gpar"]][:, :], onerow, ep_rowb[:],
                                start=True, stop=True), 1)
                epi_mm["B"] = pe.n

            @blk.scalar
            def _(sca, pend=pend):
                sca.wait_ge(pe.h, epi_mm["S"])
                act.inc(sca.activation(out=ep_row[:, 0:512],
                                       in_=accA[pend["gpar"]][0:1, 0:512],
                                       func=AF.Ln), 1)
                sca.wait_ge(act.h, act.n)
                act.inc(sca.activation(out=ep_rowb[:], in_=ep_row[:, 0:512],
                                       func=AF.Exp, scale=-1.0), 1)

            assert act.n == REC_f

            @blk.vector
            def _(v, pend=pend, AF_=(ao_free[pend["gpar"]] if pend else 0)):
                v.wait_ge(pe.h, pend["LAST_IAV"])
                dve.inc(v.tensor_copy(asum_bf[:], asum[pend["gpar"]][:]), 1)
                v.wait_ge(pe.h, epi_mm["B"])
                dve.inc(v.tensor_copy(rbcB[:], accA[pend["gpar"]][:, :]), 1)
                v.wait_ge(dve.h, dve.n)
                if AF_:
                    v.wait_ge(st_m[pend["gpar"]].h, AF_)
                dve.inc(v.tensor_tensor(out=ao[pend["gpar"]][:],
                                        in0=accat[pend["gpar"]][:],
                                        in1=rbcB[:], op=ALU.mult), 1)

            emit_pend_store(pend, EPI_F_f)

        AOC_M0, AOC_M1 = st_m[0].n, st_m[1].n

        @blk.gpsimd
        def _(g):
            g.wait_ge(st_m[0].h, AOC_M0)
            g.wait_ge(st_m[1].h, AOC_M1)
            with allow_ici_all_to_all():
                cc4.inc_default(g.collective_compute(
                    "AllToAll", ALU.bypass,
                    replica_groups=[list(range(NCORES))],
                    ins=[aoc1.opt()], outs=[aog1.opt()]))

        full_barrier()
        with contextlib.ExitStack() as pc:
            wb2 = [pc.enter_context(nc.sbuf_tensor(f"wb2{i}", [128, 2048], BF16))
                   for i in range(2)]
            wb2e = [pc.enter_context(nc.sbuf_tensor(f"wb2e{i}", [128, 2048], BF16))
                    for i in range(2)]

            with contextlib.ExitStack() as pc1:
                yb_big = pc1.enter_context(nc.sbuf_tensor("yb_big", [128, EC * 2048], BF16))
                yb_all = [yb_big[:, 2048 * i:2048 * i + 2048] for i in range(EC)]
                pat = pc1.enter_context(contextlib.ExitStack())
                at_big = pat.enter_context(
                    nc.sbuf_tensor("at_big", [128, EC * 2048], BF16))
                at_all = [at_big[:, 2048 * i:2048 * i + 2048] for i in range(EC)]

                @blk.gpsimd
                def _(g):
                    g.wait_ge(cc2.h, 1)
                    for ecc in range(EC):
                        if ecc % 4 >= 2:
                            continue
                        for b in range(2):
                            p = 16 * b + ecc
                            rr, sl = p // 4, p % 4
                            row = 256 * rr + 128 * (sl % 2)
                            ld_g.inc(g.dma_start(
                                out=at_all[ecc][:, 1024 * b:1024 * b + 1024],
                                in_=aog0[row:row + 128, :]), 16)

                ATTN_LD = ld_g.n

                @blk.sync
                def _(sy):
                    sy.wait_ge(cc4.h, 1)
                    for ecc in range(EC):
                        if ecc % 4 < 2:
                            continue
                        for b in range(2):
                            p = 16 * b + ecc
                            rr, sl = p // 4, p % 4
                            row = 256 * rr + 128 * (sl % 2)
                            ld_g2.inc(sy.dma_start(
                                out=at_all[ecc][:, 1024 * b:1024 * b + 1024],
                                in_=aog1[row:row + 128, :]), 16)

                ATTN_LD1 = ld_g2.n

                with contextlib.ExitStack() as pw:
                    xf = [pw.enter_context(nc.sbuf_tensor(f"xf{i}", [128, 1024], F32))
                          for i in range(2)]
                    t1 = [pw.enter_context(nc.sbuf_tensor(f"t1{i}", [128, 1024], F32))
                          for i in range(2)]
                    ysb = [pw.enter_context(nc.sbuf_tensor(f"ysb{i}", [128, 1024], F32))
                           for i in range(2)]
                    wop = [pw.enter_context(nc.psum_tensor(f"wop{i}", [128, 1024], F32))
                           for i in range(2)]
                    w_freeC = [0, 0]
                    p_freeC = [0, 0]
                    xf_free = [0, 0]
                    t1_free = [0, 0]
                    y_free = [0, 0]
                    xf_ready = [0, 0]

                    fi = 0
                    for fc in range(EC):
                        wbp = fc % 2

                        @blk.gpsimd
                        def _(g, fc=fc, wbp=wbp, WF=w_freeC[wbp]):
                            if WF:
                                g.wait_ge(pe.h, WF)
                            src = woutT.rearrange("(ec e) f -> e ec f", e=128)[
                                :, :, 128 * fc:128 * fc + 128]
                            dst = wb2[wbp][:].rearrange("p (ec f) -> p ec f", f=128)
                            ld_w[wbp].inc(g.dma_start(out=dst, in_=src), 16)

                        WO_LD = ld_w[wbp].n

                        for half in range(2):
                            ii = fi % 2
                            fi += 1

                            @blk.sync
                            def _(sy, fc=fc, half=half, ii=ii, XF=xf_free[ii]):
                                if XF:
                                    sy.wait_ge(dve.h, XF)
                                ld_x[ii].inc(sy.dma_start(
                                    out=xf[ii][:],
                                    in_=xT[half, 128 * fc:128 * fc + 128, :]), 16)
                            xf_ready[ii] = ld_x[ii].n

                            @blk.tensor
                            def _(t, fc=fc, half=half, ii=ii, wbp=wbp,
                                  WO_LD=WO_LD, PF=p_freeC[ii]):
                                t.wait_ge(ld_w[wbp].h, WO_LD)
                                t.wait_ge(ld_g.h, ATTN_LD)
                                if PF:
                                    t.wait_ge(act.h, PF)
                                eord = ([e for e in range(EC) if e % 4 < 2] +
                                        [e for e in range(EC) if e % 4 >= 2])
                                for h2 in range(2):
                                    colc = 1024 * half + 512 * h2
                                    for k2, e in enumerate(eord):
                                        if (fc == 0 and half == 0
                                                and h2 == 0 and k2 == 8):
                                            t.wait_ge(ld_g2.h, ATTN_LD1)
                                        inst = t.matmul(
                                            wop[ii][:, 512 * h2:512 * h2 + 512],
                                            wb2[wbp][:, 128 * e:128 * e + 128],
                                            at_all[e][:, colc:colc + 512],
                                            start=(k2 == 0),
                                            stop=(k2 == EC - 1))
                                pe.inc(inst, 1)

                            WO_FILL = pe.n
                            if half == 1:
                                w_freeC[wbp] = WO_FILL

                            @blk.scalar
                            def _(s, fc=fc, ii=ii, WO_FILL=WO_FILL,
                                  TF=t1_free[ii]):
                                s.wait_ge(pe.h, WO_FILL)
                                if TF:
                                    s.wait_ge(dve.h, TF)
                                act.inc(s.activation(
                                    out=t1[ii][:], in_=wop[ii][:], func=AF.Identity,
                                    bias=bo_sb[:, fc:fc + 1]), 1)

                            T1D = act.n
                            p_freeC[ii] = T1D

                            @blk.vector
                            def _(v, fc=fc, half=half, ii=ii, T1D=T1D,
                                  XR=xf_ready[ii], YF=y_free[ii]):
                                v.wait_ge(act.h, T1D)
                                v.wait_ge(ld_x[ii].h, XR)
                                if YF:
                                    v.wait_ge(st_y[ii].h, YF)
                                dve.inc(v.tensor_tensor(out=ysb[ii][:], in0=t1[ii][:],
                                                        in1=xf[ii][:], op=ALU.add), 1)
                                v.wait_ge(dve.h, dve.n)
                                dve.inc(v.tensor_copy(
                                    yb_all[fc][:, 1024 * half:1024 * half + 1024],
                                    ysb[ii][:]), 1)

                            xf_free[ii] = dve.n - 1
                            t1_free[ii] = dve.n - 1
                            Y_DONE = dve.n

                            @blk.sync
                            def _(sy, fc=fc, half=half, ii=ii, Y_DONE=Y_DONE):
                                sy.wait_ge(dve.h, Y_DONE)
                                st_y[ii].inc(sy.dma_start(
                                    out=yscr[128 * fc:128 * fc + 128,
                                             1024 * half:1024 * half + 1024],
                                    in_=ysb[ii][:]), 16)
                                y_free[ii] = st_y[ii].n

                pat.close()
                full_barrier()

                with contextlib.ExitStack() as p2s:
                    sqc = p2s.enter_context(nc.sbuf_tensor("sqc", [128, 8192], BF16))
                    gtmp2 = p2s.enter_context(nc.sbuf_tensor("gtmp2", [128, 8192], BF16))
                    lnrow2 = p2s.enter_context(nc.sbuf_tensor("lnrow2", [1, 512 * 7], F32))
                    lnrow2b = p2s.enter_context(nc.sbuf_tensor("lnrow2b", [1, 1024], BF16))
                    ln2bc = p2s.enter_context(nc.sbuf_tensor("ln2bc", [128, 4096], BF16))
                    st2_s = p2s.enter_context(nc.psum_tensor("st2_s", [1, 512], F32))
                    st2_s2 = p2s.enter_context(nc.psum_tensor("st2_s2", [1, 512], F32))
                    bc1c = p2s.enter_context(nc.psum_tensor("bc1c", [128, 512], F32))
                    bc2c = p2s.enter_context(nc.psum_tensor("bc2c", [128, 512], F32))

                    m_row = lnrow2[:, 0:512]
                    s2_row = lnrow2[:, 512:1024]
                    msq_row = lnrow2[:, 1024:1536]
                    var_row = lnrow2[:, 1536:2048]
                    lnt_row = lnrow2[:, 2048:2560]
                    rstd_row = lnrow2[:, 2560:3072]
                    mrs_row = lnrow2[:, 3072:3584]
                    rstd_b = lnrow2b[:, 0:512]
                    mrs_b = lnrow2b[:, 512:1024]

                    def rstd_bc2(tg):
                        return ln2bc[:, 1024 * tg:1024 * tg + 512]

                    def mrs_bc2(tg):
                        return ln2bc[:, 1024 * tg + 512:1024 * tg + 1024]

                    sq_mm = 0
                    for tg in range(NTG):
                        tgc = 512 * tg

                        @blk.vector
                        def _(v, tgc=tgc, SQMM=sq_mm):
                            if SQMM:
                                v.wait_ge(pe.h, SQMM)
                            yb3 = yb_big[:].rearrange("p (e c) -> p e c", c=2048)[
                                :, :, tgc:tgc + 512]
                            sq3 = sqc[:].rearrange("p (e t) -> p e t", t=512)
                            dve.inc(v.tensor_tensor(out=sq3, in0=yb3, in1=yb3,
                                                    op=ALU.mult), 1)

                        SQ_D = dve.n

                        @blk.tensor
                        def _(t, tg=tg, tgc=tgc, SQ_D=SQ_D):
                            t.wait_ge(dve.h, SQ_D)
                            for fc in range(EC):
                                t.matmul(st2_s[:], onecol,
                                         yb_all[fc][:, tgc:tgc + 512],
                                         start=(fc == 0), stop=(fc == EC - 1))
                            for fc in range(EC):
                                inst = t.matmul(st2_s2[:],
                                                onecol, sqc[:, 512 * fc:512 * fc + 512],
                                                start=(fc == 0), stop=(fc == EC - 1))
                            pe.inc(inst, 1)

                        sq_mm = pe.n
                        ST2_DONE = pe.n

                        @blk.vector
                        def _(v, ST2_DONE=ST2_DONE):
                            v.wait_ge(pe.h, ST2_DONE)
                            v.tensor_scalar(out=m_row, in0=st2_s[:], scalar1=C_LNSC,
                                            scalar2=None, op0=ALU.mult)
                            dve.inc(v.tensor_scalar(out=s2_row, in0=st2_s2[:],
                                                    scalar1=C_LNSC, scalar2=None,
                                                    op0=ALU.mult), 1)
                            v.wait_ge(dve.h, dve.n)
                            dve.inc(v.tensor_tensor(out=msq_row, in0=m_row, in1=m_row,
                                                    op=ALU.mult), 1)
                            v.wait_ge(dve.h, dve.n)
                            dve.inc(v.tensor_tensor(out=var_row, in0=s2_row,
                                                    in1=msq_row, op=ALU.subtract), 1)

                        V2 = dve.n

                        @blk.scalar
                        def _(sca, V2=V2):
                            sca.wait_ge(dve.h, V2)
                            act.inc(sca.activation(out=lnt_row, in_=var_row,
                                                   func=AF.Ln, bias=LN_EPS), 1)
                            sca.wait_ge(act.h, act.n)
                            act.inc(sca.activation(out=rstd_row, in_=lnt_row,
                                                   func=AF.Exp, scale=-0.5), 1)

                        RS2 = act.n

                        @blk.vector
                        def _(v, RS2=RS2):
                            v.wait_ge(act.h, RS2)
                            dve.inc(v.tensor_tensor(out=mrs_row, in0=m_row,
                                                    in1=rstd_row, op=ALU.mult), 1)
                            v.wait_ge(dve.h, dve.n)
                            v.tensor_copy(rstd_b, rstd_row)
                            dve.inc(v.tensor_copy(mrs_b, mrs_row), 1)

                        SM2 = dve.n

                        @blk.tensor
                        def _(t, SM2=SM2):
                            t.wait_ge(dve.h, SM2)
                            t.matmul(bc1c[:], onerow, rstd_b, start=True, stop=True)
                            pe.inc(t.matmul(bc2c[:], onerow, mrs_b, start=True,
                                            stop=True), 1)

                        BC2 = pe.n

                        @blk.vector
                        def _(v, tg=tg, BC2=BC2):
                            v.wait_ge(pe.h, BC2)
                            v.tensor_copy(rstd_bc2(tg), bc1c[:])
                            dve.inc(v.tensor_copy(mrs_bc2(tg), bc2c[:]), 1)

                    BC_ALL = dve.n

                    for tg in range(NTG):
                        tgc = 512 * tg

                        @blk.vector
                        def _(v, tg=tg, tgc=tgc, first=(tg == 0)):
                            v.wait_ge(dve.h, BC_ALL if first else dve.n)
                            yb3 = yb_big[:].rearrange("p (e c) -> p e c", c=2048)[
                                :, :, tgc:tgc + 512]
                            gt3 = gtmp2[:].rearrange("p (e t) -> p e t", t=512)
                            rb3 = rstd_bc2(tg).rearrange("p t -> p () t").to_broadcast(
                                (128, EC, 512))
                            mb3 = mrs_bc2(tg).rearrange("p t -> p () t").to_broadcast(
                                (128, EC, 512))
                            dve.inc(v.tensor_tensor(out=gt3, in0=yb3, in1=rb3,
                                                    op=ALU.mult), 1)
                            v.wait_ge(dve.h, dve.n)
                            dve.inc(v.tensor_tensor(out=yb3, in0=gt3, in1=mb3,
                                                    op=ALU.subtract), 1)

                G_DONE = dve.n
                g_all = yb_all
                full_barrier()

                with contextlib.ExitStack() as pg:
                    lgs = pg.enter_context(nc.sbuf_tensor("lgs", [128, 128], F32))
                    gch = pg.enter_context(nc.sbuf_tensor("gch", [128, 3 * 128 + 16], F32))
                    delta = pg.enter_context(nc.sbuf_tensor("delta", [128, 16], F32))
                    grow = pg.enter_context(nc.sbuf_tensor("grow", [128, 64], F32))
                    sumrwb = pg.enter_context(nc.sbuf_tensor("sumrwb", [128, 16], BF16))
                    srowb = pg.enter_context(nc.sbuf_tensor("srowb", [1, 2048], BF16))
                    scbc = pg.enter_context(nc.sbuf_tensor("scbc", [128, 2048], BF16))
                    pgp = pg.enter_context(contextlib.ExitStack())
                    lg_ps = pgp.enter_context(nc.psum_tensor("lg_ps", [128, 128], F32))
                    srow_ps = [pgp.enter_context(nc.psum_tensor(f"srp{i}", [1, 512], F32))
                               for i in range(4)]
                    scb_ps = pgp.enter_context(nc.psum_tensor("scb_ps", [128, 512], F32))

                    dvec = gch[:, 0:128]
                    maskc = gch[:, 128:256]
                    uc = gch[:, 256:384]
                    m1c = gch[:, 384:400]
                    e_t = grow[:, 0:16]
                    den_t = grow[:, 16:32]
                    rec_t = grow[:, 32:48]
                    t3_t = grow[:, 48:64]

                    @blk.tensor
                    def _(t):
                        t.wait_ge(dve.h, G_DONE)
                        t.wait_ge(ld_c.h, CONST_LD)
                        for w in range(16):
                            for e in range(EC):
                                inst = t.matmul(
                                    lg_ps[:, 8 * w:8 * w + 8],
                                    g_all[e][:, 128 * w:128 * w + 128],
                                    wg_sb[:, 8 * e:8 * e + 8],
                                    start=(e == 0), stop=(e == EC - 1))
                        pe.inc(inst, 1)

                    LG = pe.n

                    @blk.vector
                    def _(v, LG=LG):
                        v.wait_ge(pe.h, LG)
                        dve.inc(v.tensor_copy(lgs[:], lg_ps[:, :]), 1)
                        lg3 = lgs[:].rearrange("p (g x) -> p g x", x=8)
                        lg3t = lgs[:].rearrange("p (g x) -> p x g", x=8)
                        d3t = dvec[:].rearrange("p (g x) -> p x g", x=8)
                        mk3t = maskc[:].rearrange("p (g x) -> p x g", x=8)
                        u3t = uc[:].rearrange("p (g x) -> p x g", x=8)
                        u3 = uc[:].rearrange("p (g x) -> p g x", x=8)
                        m1b = m1c[:].rearrange("p g -> p () g").to_broadcast((128, 8, 16))
                        v.wait_ge(dve.h, dve.n)
                        dve.inc(v.tensor_reduce(out=m1c[:], in_=lg3, axis=AX.X,
                                                op=ALU.max), 1)
                        v.wait_ge(dve.h, dve.n)
                        dve.inc(v.tensor_tensor(out=d3t, in0=lg3t, in1=m1b,
                                                op=ALU.subtract), 1)
                        v.wait_ge(dve.h, dve.n)
                        dve.inc(v.tensor_scalar(out=mk3t, in0=d3t, scalar1=0.0,
                                                scalar2=None, op0=ALU.is_lt), 1)
                        v.wait_ge(dve.h, dve.n)
                        dve.inc(v.tensor_scalar(out=u3t, in0=d3t, scalar1=BIG,
                                                scalar2=None, op0=ALU.add), 1)
                        v.wait_ge(dve.h, dve.n)
                        dve.inc(v.tensor_tensor(out=u3t, in0=u3t, in1=mk3t,
                                                op=ALU.mult), 1)
                        v.wait_ge(dve.h, dve.n)
                        dve.inc(v.tensor_scalar(out=u3t, in0=u3t, scalar1=-BIG,
                                                scalar2=None, op0=ALU.add), 1)
                        v.wait_ge(dve.h, dve.n)
                        dve.inc(v.tensor_reduce(out=delta[:], in_=u3,
                                                axis=AX.X, op=ALU.max), 1)

                    DELTA = dve.n

                    @blk.scalar
                    def _(sca):
                        sca.wait_ge(dve.h, DELTA)
                        act.inc(sca.activation(out=e_t, in_=delta[:], func=AF.Exp), 1)

                    ET = act.n

                    @blk.vector
                    def _(v):
                        v.wait_ge(act.h, ET)
                        dve.inc(v.tensor_scalar(out=den_t, in0=e_t, scalar1=1.0,
                                                scalar2=None, op0=ALU.add), 1)
                        v.wait_ge(dve.h, dve.n)
                        dve.inc(v.reciprocal(out=rec_t, in_=den_t), 1)
                        v.wait_ge(dve.h, dve.n)
                        dve.inc(v.tensor_tensor(out=t3_t, in0=e_t, in1=rec_t,
                                                op=ALU.mult), 1)
                        v.wait_ge(dve.h, dve.n)
                        dve.inc(v.tensor_tensor(out=t3_t, in0=t3_t, in1=rec_t,
                                                op=ALU.add), 1)
                        v.wait_ge(dve.h, dve.n)
                        dve.inc(v.tensor_copy(sumrwb[:], t3_t), 1)

                    SRW = dve.n

                    @blk.tensor
                    def _(t):
                        t.wait_ge(dve.h, SRW)
                        for w in range(16):
                            inst = t.matmul(
                                srow_ps[w // 4][0:1, 128 * (w % 4):128 * (w % 4) + 128],
                                sumrwb[:, w:w + 1], id_sb[:], start=True, stop=True)
                        pe.inc(inst, 1)

                    SROW = pe.n

                    @blk.vector
                    def _(v):
                        v.wait_ge(pe.h, SROW)
                        for tg in range(4):
                            dve.inc(v.tensor_copy(srowb[0:1, 512 * tg:512 * tg + 512],
                                                  srow_ps[tg][0:1, :]), 1)

                    SRB = dve.n

                    for tg in range(4):
                        @blk.tensor
                        def _(t, tg=tg, SRB=SRB):
                            t.wait_ge(dve.h, SRB if tg == 0 else dve.n)
                            pe.inc(t.matmul(scb_ps[:], onerow,
                                            srowb[0:1, 512 * tg:512 * tg + 512],
                                            start=True, stop=True), 1)

                        SCB = pe.n

                        @blk.vector
                        def _(v, tg=tg, SCB=SCB):
                            v.wait_ge(pe.h, SCB)
                            dve.inc(v.tensor_copy(scbc[:, 512 * tg:512 * tg + 512],
                                                  scb_ps[:]), 1)

                    SCBC_DONE = dve.n
                    pgp.close()
                    full_barrier()

                    with contextlib.ExitStack() as ph:
                        hid_big = ph.enter_context(
                            nc.sbuf_tensor("hid_big", [128, EC * 2048], BF16))
                        hid_all = [hid_big[:, 2048 * i:2048 * i + 2048]
                                   for i in range(EC)]
                        hp = [ph.enter_context(nc.psum_tensor(f"hp{i}", [128, 1024], F32))
                              for i in range(2)]
                        hw_free = [0, 0]
                        hp_free = [0, 0]
                        fi2 = 0
                        for oc in range(EC):
                            wbp = oc % 2

                            @blk.gpsimd
                            def _(g, oc=oc, wbp=wbp, HWF=hw_free[wbp]):
                                if HWF:
                                    g.wait_ge(pe.h, HWF)
                                src = wupT.rearrange("(ec e) f -> e ec f", e=128)[
                                    :, :, 128 * oc:128 * oc + 128]
                                dst = wb2[wbp][:].rearrange("p (ec f) -> p ec f", f=128)
                                ld_w[wbp].inc(g.dma_start(out=dst, in_=src), 16)

                            HW_LD = ld_w[wbp].n

                            for half in range(2):
                                ii = fi2 % 2
                                fi2 += 1

                                @blk.tensor
                                def _(t, oc=oc, half=half, ii=ii, wbp=wbp,
                                      HW_LD=HW_LD, HPF=hp_free[ii]):
                                    t.wait_ge(ld_w[wbp].h, HW_LD)
                                    if HPF:
                                        t.wait_ge(dve.h, HPF)
                                    for h2 in range(2):
                                        colc = 1024 * half + 512 * h2
                                        for e in range(EC):
                                            inst = t.matmul(
                                                hp[ii][:, 512 * h2:512 * h2 + 512],
                                                wb2[wbp][:, 128 * e:128 * e + 128],
                                                g_all[e][:, colc:colc + 512],
                                                start=(e == 0), stop=(e == EC - 1))
                                    pe.inc(inst, 1)

                                HP_FILL = pe.n
                                if half == 1:
                                    hw_free[wbp] = HP_FILL

                                @blk.vector
                                def _(v, oc=oc, half=half, ii=ii, HP_FILL=HP_FILL):
                                    v.wait_ge(pe.h, HP_FILL)
                                    dve.inc(v.tensor_scalar(
                                        out=hid_all[oc][:, 1024 * half:1024 * half + 1024],
                                        in0=hp[ii][:], scalar1=0.0, scalar2=None,
                                        op0=ALU.max), 1)

                                hp_free[ii] = dve.n

                        HID_DONE = dve.n

                        with nc.sbuf_tensor("gs_t", [128, 8192], BF16) as gs_t:
                            for tg in range(NTG):
                                tgc = 512 * tg

                                @blk.vector
                                def _(v, tg=tg, tgc=tgc, first2=(tg == 0)):
                                    if first2:
                                        v.wait_ge(dve.h, HID_DONE)
                                        v.wait_ge(dve.h, SCBC_DONE)
                                    else:
                                        v.wait_ge(dve.h, dve.n)
                                    g3 = yb_big[:].rearrange("p (e c) -> p e c", c=2048)[
                                        :, :, tgc:tgc + 512]
                                    gs3 = gs_t[:].rearrange("p (e t) -> p e t", t=512)
                                    sc3 = scbc[:, tgc:tgc + 512].rearrange(
                                        "p t -> p () t").to_broadcast((128, EC, 512))
                                    dve.inc(v.tensor_tensor(out=gs3, in0=g3, in1=sc3,
                                                            op=ALU.mult), 1)
                                    v.wait_ge(dve.h, dve.n)
                                    dve.inc(v.tensor_copy(g3, gs3), 1)

                        GS_DONE = dve.n
                        full_barrier()

                        with contextlib.ExitStack() as p5:
                            yr = [p5.enter_context(
                                nc.sbuf_tensor(f"yr{i}", [128, 1024], F32))
                                for i in range(2)]
                            ot = [p5.enter_context(
                                nc.sbuf_tensor(f"ot{i}", [128, 1024], F32))
                                for i in range(2)]
                            fo = [p5.enter_context(
                                nc.psum_tensor(f"fo{i}", [128, 1024], F32))
                                for i in range(2)]
                            fw_free = [0, 0]
                            fo_free = [0, 0]
                            yr_free = [0, 0]
                            yr_ready = [0, 0]
                            ot_free = [0, 0]
                            YD0, YD1 = st_y[0].n, st_y[1].n
                            fi3 = 0
                            for fc in range(EC):
                                wbp = fc % 2

                                @blk.gpsimd
                                def _(g, fc=fc, wbp=wbp, FWF=fw_free[wbp]):
                                    if FWF:
                                        g.wait_ge(pe.h, FWF)
                                    src = wdnT.rearrange("(ec e) f -> e ec f", e=128)[
                                        :, :, 128 * fc:128 * fc + 128]
                                    dst = wb2[wbp][:].rearrange("p (ec f) -> p ec f", f=128)
                                    ld_w[wbp].inc(g.dma_start(out=dst, in_=src), 16)
                                    src2 = wexp.rearrange("(ec e) f -> e ec f", e=128)[
                                        :, :, 128 * fc:128 * fc + 128]
                                    dst2 = wb2e[wbp][:].rearrange("p (ec f) -> p ec f", f=128)
                                    ld_w[wbp].inc(g.dma_start(out=dst2, in_=src2), 16)

                                FW_LD = ld_w[wbp].n

                                for half in range(2):
                                    ii = fi3 % 2
                                    fi3 += 1

                                    @blk.sync
                                    def _(sy, fc=fc, half=half, ii=ii,
                                          YRF=yr_free[ii]):
                                        if YRF:
                                            sy.wait_ge(dve.h, YRF)
                                        if fc == 0 and half == 0:
                                            sy.wait_ge(st_y[0].h, YD0)
                                            sy.wait_ge(st_y[1].h, YD1)
                                        ld_x[ii].inc(sy.dma_start(
                                            out=yr[ii][:],
                                            in_=yscr[128 * fc:128 * fc + 128,
                                                     1024 * half:1024 * half + 1024]), 16)
                                    yr_ready[ii] = ld_x[ii].n

                                    @blk.tensor
                                    def _(t, fc=fc, half=half, ii=ii, wbp=wbp,
                                          FW_LD=FW_LD, FOF=fo_free[ii]):
                                        t.wait_ge(ld_w[wbp].h, FW_LD)
                                        if fc == 0 and half == 0:
                                            t.wait_ge(dve.h, GS_DONE)
                                        if FOF:
                                            t.wait_ge(dve.h, FOF)
                                        for h2 in range(2):
                                            colc = 1024 * half + 512 * h2
                                            for e in range(EC):
                                                t.matmul(
                                                    fo[ii][:, 512 * h2:512 * h2 + 512],
                                                    wb2[wbp][:, 128 * e:128 * e + 128],
                                                    hid_all[e][:, colc:colc + 512],
                                                    start=(e == 0), stop=False)
                                            for e in range(EC):
                                                inst = t.matmul(
                                                    fo[ii][:, 512 * h2:512 * h2 + 512],
                                                    wb2e[wbp][:, 128 * e:128 * e + 128],
                                                    g_all[e][:, colc:colc + 512],
                                                    start=False, stop=(e == EC - 1))
                                        pe.inc(inst, 1)

                                    FO_FILL = pe.n
                                    if half == 1:
                                        fw_free[wbp] = FO_FILL

                                    @blk.vector
                                    def _(v, ii=ii, FO_FILL=FO_FILL,
                                          YRR=yr_ready[ii], OTF=ot_free[ii]):
                                        v.wait_ge(pe.h, FO_FILL)
                                        v.wait_ge(ld_x[ii].h, YRR)
                                        if OTF:
                                            v.wait_ge(st_o[ii].h, OTF)
                                        dve.inc(v.tensor_tensor(
                                            out=ot[ii][:], in0=fo[ii][:],
                                            in1=yr[ii][:], op=ALU.add), 1)

                                    yr_free[ii] = dve.n
                                    fo_free[ii] = dve.n
                                    OT_D = dve.n

                                    @blk.scalar
                                    def _(sca, fc=fc, half=half, ii=ii, OT_D=OT_D):
                                        sca.wait_ge(dve.h, OT_D)
                                        st_o[ii].inc(sca.dma_start(
                                            out=outT[half, 128 * fc:128 * fc + 128, :],
                                            in_=ot[ii][:]), 16)
                                        ot_free[ii] = st_o[ii].n

        FIN0, FIN1 = st_o[0].n, st_o[1].n

        @blk.sync
        def _(sy):
            sy.wait_ge(st_o[0].h, FIN0)
            sy.wait_ge(st_o[1].h, FIN1)

        @blk.gpsimd
        def _(g):
            g.wait_ge(st_o[0].h, FIN0)
            g.wait_ge(st_o[1].h, FIN1)

    return nc


def _host_constants():
    dim = E
    inv_freq = np.float32(ROPE_BASE) ** (
        -(np.arange(0, dim, 2, dtype=np.float32) / np.float32(dim)))
    emb = np.repeat(inv_freq.astype(np.float32), 2)
    arg = (np.float32(S) * emb).astype(np.float32)
    sin = np.sin(arg).astype(np.float32)
    cos = np.cos(arg).astype(np.float32)

    mats = np.zeros((48, 128, 128), np.float32)
    for c in range(16):
        dcos = mats[3 * c]
        m1 = mats[3 * c + 1]
        m2 = mats[3 * c + 2]
        for i in range(128):
            p = 128 * c + i
            dcos[i, i] = cos[p]
            if c < 8:
                if i < 64:
                    m1[2 * i + 1, i] = -sin[p]
                else:
                    m2[2 * (i - 64) + 1, i] = -sin[p]
            else:
                if i < 64:
                    m1[2 * i, i] = sin[p]
                else:
                    m2[2 * (i - 64), i] = sin[p]
    mats_bf = mats.astype(ml_dtypes.bfloat16)

    masks = np.zeros((2, 128, 1024), np.float32)
    n = np.arange(512)
    p = np.arange(128)[:, None]
    for mi in range(2):
        for half in range(2):
            o = 256 * mi + 128 * half
            masks[mi, :, 512 * half:512 * half + 512] = (n[None, :] >= p + o)
    masks_bf = masks.astype(ml_dtypes.bfloat16)

    ident = np.eye(128, dtype=ml_dtypes.bfloat16)
    ones = np.ones((128, 128), dtype=ml_dtypes.bfloat16)
    cst = np.zeros((128, 8), np.float32)
    cst[:, 0] = np.float32(C_LNSC)
    cst[:, 1] = np.float32(LN_EPS)
    cst[:, 2] = np.float32(-0.5)
    cst[:, 3] = np.float32(0.5)
    return mats_bf, masks_bf, ident, ones, cst


def kernel(L0_attn_x, L0_attn_Wqkv, L0_attn_bqkv, L0_attn_Wout, L0_attn_bout,
           L0_ffn_W_up, L0_ffn_W_down, L0_ffn_W_gate, L0_ffn_W_expert):
    x = np.asarray(L0_attn_x, np.float32)
    mats_bf, masks_bf, ident, ones, cst = _host_constants()

    wqkvT = np.ascontiguousarray(np.asarray(L0_attn_Wqkv, np.float32).T)
    woutT = np.ascontiguousarray(np.asarray(L0_attn_Wout, np.float32).T)
    wupT = np.ascontiguousarray(np.asarray(L0_ffn_W_up, np.float32).T)
    wdnT = np.ascontiguousarray(np.asarray(L0_ffn_W_down, np.float32).T)
    wexp_ = np.ascontiguousarray(np.asarray(L0_ffn_W_expert, np.float32))
    wgT = np.ascontiguousarray(np.asarray(L0_ffn_W_gate, np.float32).T)
    bqkv2 = np.ascontiguousarray(
        np.asarray(L0_attn_bqkv, np.float32).reshape(48, 128).T)
    bout2 = np.ascontiguousarray(
        np.asarray(L0_attn_bout, np.float32).reshape(16, 128).T)

    xT = np.ascontiguousarray(x.transpose(0, 2, 1))

    nc = build()
    shared = {
        "wqkvT": wqkvT, "woutT": woutT, "wupT": wupT, "wdnT": wdnT,
        "wexp": wexp_, "wgT": wgT, "bqkv2": bqkv2, "bout2": bout2,
        "ropem": mats_bf, "masksd": masks_bf, "identd": ident,
        "onesd": ones, "cstd": cst,
    }
    in_maps = []
    for c in range(NCORES):
        m = dict(shared)
        m["xT"] = np.ascontiguousarray(xT[:, :, PB * c:PB * c + PB])
        in_maps.append(m)

    res = run_bass_kernel_spmd(nc, in_maps, core_ids=list(range(NCORES)))

    out = np.empty((B, S, E), np.float32)
    for c in range(NCORES):
        o = res.results[c]["outT"]
        out[:, PB * c:PB * c + PB, :] = np.asarray(o).transpose(0, 2, 1)
    if _DBG:
        kernel._dbg = [np.asarray(res.results[c]["dbg"]) for c in range(NCORES)]
    return out
